# version 11
# baseline (speedup 1.0000x reference)
import numpy as np
import ml_dtypes
import concourse.bass as bass
import concourse.mybir as mybir
from concourse.bass_utils import run_bass_kernel_spmd

F32 = mybir.dt.float32
BF16 = mybir.dt.bfloat16
AF = mybir.ActivationFunctionType
ALU = mybir.AluOpType
AXX = mybir.AxisListType.X

S = 8192
SQ = 2048
D = 1024
DH = 64
H = 8
E = 5
NT = 16
GROUPS = [[0, 1, 2, 3], [4, 5, 6, 7]]
ENGS = ("sync", "gpsimd", "vector", "scalar", "tensor")
KT_ELS = 512 * SQ
V_ELS = SQ * (H * 65)
BLOB_ELS = KT_ELS + V_ELS
KB = 1024

BUILD_PHASES = 4


class Sched:
    def __init__(self, nc):
        self.nc = nc
        self.q = {e: [] for e in ENGS}
        self.sems = {}
        self.cnt = {}

    def sem(self, name):
        if name not in self.sems:
            self.sems[name] = self.nc.semaphore(name).__enter__()
            self.cnt[name] = 0
        return self.sems[name]

    def on(self, eng, fn):
        self.q[eng].append(fn)

    def wait(self, eng, sem, val):
        if not val:
            return
        h = self.sem(sem)
        self.on(eng, lambda E, h=h, v=val: E.wait_ge(h, v))

    def do(self, eng, fn, sem=None, inc=1):
        if sem is not None:
            h = self.sem(sem)
            self.cnt[sem] += inc
            v = self.cnt[sem]
            self.on(eng, lambda E, fn=fn, h=h, inc=inc: fn(E).then_inc(h, inc))
            return v
        self.on(eng, lambda E, fn=fn: fn(E))
        return None

    ENGSEM = {"vector": "v", "scalar": "s", "tensor": "t", "gpsimd": "g", "sync": "y"}

    def dep(self, eng, fn):
        m = self.ENGSEM[eng]
        self.wait(eng, m, self.cnt.get(m, 0))
        return self.do(eng, fn, m)

    def dma(self, eng, out, in_, sem="gd"):
        return self.do(eng, lambda E, o=out, i=in_: E.dma_start(out=o, in_=i), sem, 16)

    def barrier(self):
        self.wait("gpsimd", "gd", self.cnt.get("gd", 0))
        h = self.sem("bar")
        self.cnt["bar"] += len(ENGS)
        v = self.cnt["bar"]
        for e in ENGS:
            self.on(e, lambda E, h=h: E.nop().then_inc(h, 1))
            self.on(e, lambda E, h=h, v=v: E.wait_ge(h, v))

    def replay(self):
        with self.nc.Block() as block:
            @block.sync
            def _(E):
                for fn in self.q["sync"]:
                    fn(E)

            @block.gpsimd
            def _(E):
                for fn in self.q["gpsimd"]:
                    fn(E)

            @block.vector
            def _(E):
                for fn in self.q["vector"]:
                    fn(E)

            @block.scalar
            def _(E):
                for fn in self.q["scalar"]:
                    fn(E)

            @block.tensor
            def _(E):
                for fn in self.q["tensor"]:
                    fn(E)


def bview(arena, byte_off, dtype, shape):
    esz = mybir.dt.size(dtype)
    assert byte_off % esz == 0
    ap = arena[:].bitcast(dtype)
    parts, cols = shape[0], int(np.prod(shape[1:]))
    v = ap[0:parts, byte_off // esz: byte_off // esz + cols]
    if len(shape) > 2:
        names = " ".join(f"d{i}" for i in range(1, len(shape)))
        v = v.rearrange(f"p ({names}) -> p {names}",
                        **{f"d{i}": shape[i] for i in range(1, len(shape))})
    return v



def _emit_av(sc, W, bank, vr, pr, qTp, accF, CHF, ci, hl, hp, s_exp_at, t_av_buf, Ab, v_accf):
    j0, nj = CHF[ci]
    bav = 6 + hl
    h = hp * 2 + hl
    abuf = Ab[hl][ci % 2]
    W("tensor", "s", s_exp_at[(hl, ci)])
    if j0 == 0:
        W("tensor", "v", v_accf)
    ALU = mybir.AluOpType
    t_av = 0
    for jj in range(nj):
        t_av = sc.do("tensor", lambda E, bav=bav, abuf=abuf, jj=jj, j0=j0, h=h, pr=pr: E.matmul(
            bank(bav, p0=0, p1=65), vr[pr][:, j0 + jj, h * 65: h * 65 + 65],
            abuf[:, jj * 512:(jj + 1) * 512],
            start=(j0 == 0 and jj == 0), stop=(j0 + jj == 15)), "t")
    t_av_buf[(hl, ci % 2)] = t_av


def build(debug=(), phases=BUILD_PHASES, acut=99):
    nc = bass.Bass("TRN2", target_bir_lowering=False, debug=False)

    xT = nc.dram_tensor("xT", [D, S], F32, kind="ExternalInput").ap()
    xqT = nc.dram_tensor("xqT", [D, SQ], F32, kind="ExternalInput").ap()
    xq = nc.dram_tensor("xq", [SQ, D], F32, kind="ExternalInput").ap()
    WaT = nc.dram_tensor("WaT", [D, 192], F32, kind="ExternalInput").ap()
    Wsd = nc.dram_tensor("Wsd", [D, 80], F32, kind="ExternalInput").ap()
    Wqf = nc.dram_tensor("Wqf", [D, H * E * DH], F32, kind="ExternalInput").ap()
    Wkvf = nc.dram_tensor("Wkvf", [D, 2 * H * E * DH], F32, kind="ExternalInput").ap()
    WoB = nc.dram_tensor("WoB", [H * E * DH, D], F32, kind="ExternalInput").ap()
    out = nc.dram_tensor("out", [SQ, D], F32, kind="ExternalOutput").ap()

    dbg = {}
    def dbg_t(name, shape, dtype):
        if name in debug:
            dbg[name] = nc.dram_tensor("dbg_" + name, shape, dtype, kind="ExternalOutput").ap()

    dbg_t("acc", [128, 4096], F32)
    dbg_t("oat", [128, NT * DH], F32)
    dbg_t("y", [SQ, D], F32)
    dbg_t("gates", [128, NT * 80], F32)
    dbg_t("mdd", [128, NT * 40], F32)
    dbg_t("kTown", [128, 4 * SQ], BF16)
    dbg_t("vown", [128, NT * H * 65], BF16)
    dbg_t("accF", [65, 16384], F32)
    dbg_t("oaf", [128, NT * H * DH], BF16)
    dbg_t("rzt", [128, NT * H], F32)
    dbg_t("mh", [128, 8], F32)

    ydram = nc.dram_tensor("ydram", [SQ, D], F32).ap()
    rs_in = nc.dram_tensor("rs_in", [256, SQ], F32).ap()
    rs_out = nc.dram_tensor("rs_out", [64, SQ], F32).ap()
    ag_in = nc.dram_tensor("ag_in", [1, BLOB_ELS], BF16).ap()
    ag_out = nc.dram_tensor("ag_out", [4, BLOB_ELS], BF16).ap()

    ar1 = nc.sbuf_tensor("ar1", [128, 100 * KB], mybir.dt.uint8).__enter__()
    ar2 = nc.sbuf_tensor("ar2", [128, 64 * KB], mybir.dt.uint8).__enter__()
    ar3 = nc.sbuf_tensor("ar3", [128, 38 * KB], mybir.dt.uint8).__enter__()
    ps = nc.psum_tensor("ps", [128, 4096], F32).__enter__()
    psB = ps[:].bitcast(BF16)

    def bank(i, p0=0, p1=128, cols=512, off=0):
        return ps[p0:p1, i * 512 + off: i * 512 + off + cols]

    def bankb(i, p0=0, p1=128, cols=512, off=0):
        return psB[p0:p1, i * 1024 + off: i * 1024 + off + cols]

    sc = Sched(nc)
    W = sc.wait
    cast_m = {0: 0, 1: 0}

    def staged(src, dst, e, st_view):
        par = e % 2
        sem = f"ld{par}"
        W("sync", "g", cast_m[par])
        v = sc.dma("sync", st_view, src, sem)
        W("gpsimd", sem, v)
        cast_m[par] = sc.do("gpsimd", lambda E, dst=dst, st=st_view: E.tensor_copy(dst, st), "g")
        return cast_m[par]

    ident_f = bview(ar3, 32 * KB, F32, [128, 128])
    ident_b = bview(ar3, 32 * KB + 512, BF16, [128, 128])
    ones_b = bview(ar3, 32 * KB + 768, BF16, [1, 128])
    onec_b = bview(ar3, 32 * KB + 1024, BF16, [128, 1])
    sm = 34 * KB
    qmaxb = bview(ar3, sm, F32, [128, 1])
    epsc = bview(ar3, sm + 4000, F32, [128, 1])
    kn2 = bview(ar3, sm + 8, F32, [128, 16])
    knr = bview(ar3, sm + 72, F32, [128, 16])
    negmk = bview(ar3, sm + 136, F32, [128, 16])
    zp = bview(ar3, sm + 200, F32, [128, 8])
    zk = bview(ar3, sm + 232, F32, [128, 1])
    rzk = bview(ar3, sm + 236, F32, [128, 1])
    lnA = bview(ar3, sm + 240, F32, [128, 1])
    lnB = bview(ar3, sm + 244, F32, [128, 1])
    lnC = bview(ar3, sm + 248, F32, [128, 1])
    lnD = bview(ar3, sm + 252, F32, [128, 1])
    lnG = bview(ar3, sm + 256, F32, [128, 1])
    lnE_ = bview(ar3, sm + 260, F32, [128, 1])
    lnF = bview(ar3, sm + 264, F32, [128, 1])
    mh = bview(ar3, sm + 272, F32, [128, 8])
    nq = bview(ar3, sm + 304, F32, [128, 8])
    nk = bview(ar3, sm + 336, F32, [128, 8])
    nqt = bview(ar3, sm + 368, F32, [128, 2])
    nkt = bview(ar3, sm + 1000, F32, [128, 2])
    q1 = bview(ar3, sm + 376, F32, [1, 1])
    q2 = bview(ar3, sm + 380, F32, [1, 1])
    q3 = bview(ar3, sm + 384, BF16, [1, 1])
    qrow = bview(ar3, sm + 388, F32, [1, 4])
    s8a = bview(ar3, sm + 512, F32, [8, 128])
    s8b = bview(ar3, sm + 1536, F32, [8, 128])
    r8a = bview(ar3, sm + 2560, F32, [8, 1])
    r8b = bview(ar3, sm + 2564, F32, [8, 1])
    row8 = bview(ar3, sm + 2568, F32, [1, 8])
    row8b = bview(ar3, sm + 2600, F32, [1, 8])
    row8c = bview(ar3, sm + 2632, BF16, [1, 8])
    ztok = bview(ar3, sm + 2648, F32, [128, NT, 8])
    rztok = bview(ar3, sm + 3160, F32, [128, NT, 8])

    ssg = bview(ar3, 0, F32, [128, NT, 40])
    sdg = bview(ar3, 2560 * 1, F32, [128, NT, 40])
    mdA = bview(ar3, 2560 * 2, F32, [128, NT, 40])
    mdz = bview(ar3, 2560 * 3, F32, [128, NT, 40])
    sgraw = bview(ar3, 2560 * 4, F32, [128, 80])
    sgsig = bview(ar3, 2560 * 4 + 320, F32, [128, 80])
    med = bview(ar3, 2560 * 4 + 640, F32, [128, 6, 8])
    gaccq2 = [bview(ar3, 2560 * 4 + 832, F32, [128, 64]), bview(ar3, 12352, F32, [128, 64])]
    gacck2 = [bview(ar3, 2560 * 4 + 1088, F32, [128, 64]), bview(ar3, 12608, F32, [128, 64])]
    gaccv2 = [bview(ar3, 2560 * 4 + 1344, F32, [128, 64]), bview(ar3, 12864, F32, [128, 64])]
    sqscr = bview(ar3, 2560 * 4 + 1600, F32, [128, 64])
    sqscr2 = bview(ar3, 2560 * 4 + 1856, F32, [128, 64])

    v = sc.do("gpsimd", lambda E: E.memset(ident_f, 0.0), "g")
    v = sc.do("gpsimd", lambda E: E.memset(ident_b, 0.0), "g")
    v = sc.do("gpsimd", lambda E: E.memset(ones_b, 1.0), "g")
    v = sc.do("gpsimd", lambda E: E.memset(onec_b, 1.0), "g")
    v = sc.do("gpsimd", lambda E: E.memset(epsc, 1e-5), "g")
    W("gpsimd", "g", v)
    v = sc.do("gpsimd", lambda E: E.affine_select(
        out=ident_f, in_=ident_f, compare_op=ALU.not_equal,
        fill=1.0, base=0, pattern=[[-1, 128]], channel_multiplier=1), "g")
    g_setup = sc.do("gpsimd", lambda E: E.affine_select(
        out=ident_b, in_=ident_b, compare_op=ALU.not_equal,
        fill=1.0, base=0, pattern=[[-1, 128]], channel_multiplier=1), "g")
    W("gpsimd", "g", g_setup)

    stg = [bview(ar1, 0, F32, [128, 2048]), bview(ar1, 8 * KB, F32, [128, 2048])]
    xbbf = bview(ar1, 16 * KB, BF16, [128, 8, 2048])
    QT = bview(ar1, 48 * KB, BF16, [128, 8192])
    acc = bview(ar1, 64 * KB, F32, [128, 4096])
    Apr = bview(ar1, 80 * KB, BF16, [128, 8192])
    KT = bview(ar2, 0, BF16, [128, 2048])
    Ktok = bview(ar2, 4 * KB, BF16, [128, 16, 64])
    Vtok = bview(ar2, 6 * KB, BF16, [128, 16, 64])
    Vhat = bview(ar2, 8 * KB, BF16, [128, 64])
    WaTb = bview(ar2, 9 * KB, BF16, [128, 8, 192])
    sqb = bview(ar2, 13 * KB, BF16, [64, 2048])
    oat = bview(ar2, 18 * KB, F32, [128, NT, 64])
    arstage = bview(ar2, 23 * KB, F32, [64, 2048])

    if acut <= -1:
        _finish(sc)
        return nc, dbg
    for e in range(8):
        staged(WaT[e * 128:(e + 1) * 128, :], WaTb[:, e, :], e, stg[e % 2][:, 0:192])

    if acut <= 0:
        _finish(sc)
        return nc, dbg
    for e in range(8):
        staged(xqT[e * 128:(e + 1) * 128, :], xbbf[:, e, :], e, stg[e % 2])
    v_cast = cast_m[1]
    if acut <= 0.3:
        _finish(sc)
        return nc, dbg
    W("tensor", "g", v_cast)
    W("tensor", "g", g_setup)
    ev_bank = {}
    tr_ev = 0
    s_kn = 0
    for blk in range(16):
        pj = blk % 4
        W("tensor", "v", ev_bank.get(pj, 0))
        W("tensor", "s", s_kn if blk >= 4 else 0)
        for e in range(8):
            tmm = sc.do("tensor", lambda E, e=e, blk=blk, pj=pj: E.matmul(
                bank(pj, cols=128), xbbf[:, e, blk * 128:(blk + 1) * 128],
                WaTb[:, e, 64:192], start=(e == 0), stop=(e == 7)), "t")
        W("vector", "t", tmm)
        sc.do("vector", lambda E, pj=pj, blk=blk: E.tensor_copy(
            Ktok[:, blk, :], bank(pj, cols=64)), "v")
        ev = sc.do("vector", lambda E, pj=pj, blk=blk: E.tensor_copy(
            Vtok[:, blk, :], bank(pj, cols=64, off=64)), "v")
        ev_bank[pj] = ev
        if acut > 0.8:
            W("tensor", "v", ev)
            t = sc.do("tensor", lambda E, blk=blk: E.transpose(
                bankb(5, p1=64, cols=128), Ktok[:, blk, :], ident_b), "t")
            W("vector", "t", t)
            tr_ev = sc.do("vector", lambda E, blk=blk: E.tensor_copy(
                KT[0:64, blk * 128: blk * 128 + 128], bankb(5, p1=64, cols=128)), "v")
            W("tensor", "v", tr_ev)
    s_kn_done = s_kn
    if acut <= 1:
        _finish(sc)
        return nc, dbg
    W("tensor", "s", s_kn_done)

    t_qband = sc.cnt["t"]
    v_qev = 0
    s_sq = 0
    t_qn = 0
    v_qn = 0
    qrow_init = True
    for band in range(4):
        for e in range(8):
            if e == 0:
                W("gpsimd", "t", t_qband)
            staged(xT[e * 128:(e + 1) * 128, band * 2048:(band + 1) * 2048],
                   xbbf[:, e, :], e, stg[e % 2])
        W("tensor", "g", cast_m[e % 2])
        W("tensor", "v", v_qev)
        W("tensor", "s", s_sq)
        for sub in range(4):
            for e in range(8):
                t_qband = sc.do("tensor", lambda E, sub=sub, e=e: E.matmul(
                    bank(sub, p1=64), WaTb[:, e, 0:64], xbbf[:, e, sub * 512:(sub + 1) * 512],
                    start=(e == 0), stop=(e == 7)), "t")
        W("vector", "t", t_qband)
        for sub in range(4):
            v_qev = sc.do("vector", lambda E, band=band, sub=sub: E.tensor_copy(
                QT[0:64, band * 2048 + sub * 512: band * 2048 + (sub + 1) * 512],
                bank(sub, p1=64)), "v")
        s_sq = s_sq
    if acut <= 2:
        _finish(sc)
        return nc, dbg
    if acut <= 3:
        _finish(sc)
        return nc, dbg
    CH = [(0, 1536), (1536, 1536), (3072, 1536), (4608, 1536), (6144, 1536), (7680, 512)]
    s_exp_set = {0: 0, 3: 0}
    v_rmx_set = {0: 0, 3: 0}
    W("tensor", "v", sc.cnt["v"])
    av_evac = sc.cnt["v"]
    prev_avv = {6: 0, 7: 0}
    rmxp = zp
    for kblk in range(16):
        for ci, (ibase, fd) in enumerate(CH):
            b0 = 0 if ci % 2 == 0 else 3
            W("tensor", "s", s_exp_set[b0])
            W("tensor", "v", v_rmx_set[b0])
            for sub in range(fd // 512):
                t = sc.do("tensor", lambda E, kblk=kblk, b0=b0, sub=sub, ibase=ibase: E.matmul(
                    bank(b0 + sub), KT[0:64, kblk * 128:(kblk + 1) * 128],
                    QT[0:64, ibase + sub * 512: ibase + (sub + 1) * 512],
                    start=True, stop=True), "t")
            W("vector", "t", t)
            v_rmx_set[b0] = sc.dep("vector", lambda E, b0=b0, fd=fd, ci=ci: E.reduce_max(
                knr[:, ci: ci + 1], bank(b0, cols=fd), axis=AXX))
        v = sc.dep("vector", lambda E: E.reduce_max(rzk[:], knr[:, 0:6], axis=AXX))
        v_negm = sc.dep("vector", lambda E, kblk=kblk: E.tensor_scalar(
            negmk[:, kblk: kblk + 1], rzk[:], -1.0, None, ALU.mult))
        W("scalar", "v", v_negm)
        for ci, (ibase, fd) in enumerate(CH):
            b0 = 0 if ci % 2 == 0 else 3
            cidx = kblk * 6 + ci
            if ci >= 2:
                W("tensor", "s", s_exp_set[b0])
            else:
                W("tensor", "v", av_evac)
            W("tensor", "v", v_rmx_set[b0])
            for sub in range(fd // 512):
                t = sc.do("tensor", lambda E, kblk=kblk, b0=b0, sub=sub, ibase=ibase: E.matmul(
                    bank(b0 + sub), KT[0:64, kblk * 128:(kblk + 1) * 128],
                    QT[0:64, ibase + sub * 512: ibase + (sub + 1) * 512],
                    start=True, stop=True), "t")
            W("scalar", "t", t)
            s_exp_set[b0] = sc.do("scalar", lambda E, b0=b0, fd=fd, ibase=ibase, ci=ci, kblk=kblk: E.activation(
                Apr[:, ibase: ibase + fd], bank(b0, cols=fd), AF.Exp,
                bias=negmk[:, kblk: kblk + 1], accum_out=zp[:, ci: ci + 1]), "s")
        W("vector", "s", s_exp_set[b0])
        sc.dep("vector", lambda E: E.reduce_sum(zk[:], zp[:, 0:6], axis=AXX))
        v_rz = sc.dep("vector", lambda E: E.reciprocal(rzk[:], zk[:]))
        W("scalar", "v", v_rz)
        s_vh = sc.do("scalar", lambda E, kblk=kblk: E.mul(Vhat[:], Vtok[:, kblk, :], rzk[:, 0:1]), "s")
        W("tensor", "s", s_vh)
        for pr in range(8):
            bav = 6 + (pr % 2)
            W("tensor", "v", prev_avv[bav])
            t1 = sc.do("tensor", lambda E, bav=bav, pr=pr: E.matmul(
                bank(bav, p0=0, p1=64), Vhat[:], Apr[:, (2 * pr) * 512:(2 * pr + 1) * 512],
                start=True, stop=True, tile_position=(0, 0)), "t")
            t2 = sc.do("tensor", lambda E, bav=bav, pr=pr: E.matmul(
                bank(bav, p0=64, p1=128), Vhat[:], Apr[:, (2 * pr + 1) * 512:(2 * pr + 2) * 512],
                start=True, stop=True, tile_position=(0, 64)), "t")
            W("vector", "t", t2)
            if kblk == 0:
                ev = sc.do("vector", lambda E, bav=bav, pr=pr: E.tensor_copy(
                    acc[:, pr * 512:(pr + 1) * 512], bank(bav)), "v")
            else:
                ev = sc.do("vector", lambda E, bav=bav, pr=pr: E.tensor_tensor(
                    acc[:, pr * 512:(pr + 1) * 512], bank(bav),
                    acc[:, pr * 512:(pr + 1) * 512], ALU.add), "v")
            prev_avv[bav] = ev
        av_evac = sc.cnt["v"]

    if acut <= 4:
        _finish(sc)
        return nc, dbg
    if "acc" in dbg:
        sc.wait("gpsimd", "v", av_evac)
        sc.dma("gpsimd", dbg["acc"][:, :], acc[:], "gd")

    W("gpsimd", "v", av_evac)
    for qr in range(4):
        for prl in range(2):
            for par in range(2):
                g = sc.dma("gpsimd",
                           rs_in[qr * 64:(qr + 1) * 64, prl * 1024 + par * 512: prl * 1024 + (par + 1) * 512],
                           acc[par * 64:(par + 1) * 64, (2 * qr + prl) * 512:(2 * qr + prl + 1) * 512],
                           "gd")
    W("gpsimd", "gd", g)
    g_cc = sc.do("gpsimd", lambda E: E.collective_compute(
        "ReduceScatter", ALU.add, replica_groups=GROUPS,
        ins=[rs_in.opt()], outs=[rs_out.opt()]), "cc", 1)

    if acut <= 5:
        _finish(sc)
        return nc, dbg
    W("sync", "cc", g_cc)
    v = sc.dma("sync", arstage[:, :], rs_out[:, :], "ldm")
    W("tensor", "ldm", v)
    for tt in range(NT):
        t = sc.do("tensor", lambda E, tt=tt: E.transpose(
            bank(0, cols=64, off=(tt % 4) * 64), arstage[0:64, tt * 128:(tt + 1) * 128],
            ident_f[0:64, 0:64]), "t")
        if tt % 4 == 3:
            W("vector", "t", t)
            for u in range(4):
                v = sc.do("vector", lambda E, tt=tt, u=u: E.tensor_copy(
                    oat[:, tt - 3 + u, :], bank(0, cols=64, off=u * 64)), "v")
            W("tensor", "v", v)
    if "oat" in dbg:
        W("gpsimd", "v", sc.cnt["v"])
        sc.dma("gpsimd", dbg["oat"][:, :], oat[:].rearrange("p a b -> p (a b)"), "gd")

    sc.barrier()
    if phases < 2:
        _finish(sc)
        return nc, dbg

    xq_t = bview(ar1, 0, F32, [128, 1024])
    z1 = bview(ar1, 4 * KB, F32, [128, 1024])
    sqz = bview(ar1, 8 * KB, F32, [128, 1024])
    ybf_t = bview(ar1, 12 * KB, BF16, [128, 1024])
    yf_t = bview(ar1, 14 * KB, F32, [128, 1024])
    yT = bview(ar2, 32 * KB, BF16, [128, 8, 2048])

    xqr = xq.rearrange("(a p) d -> a p d", p=128)
    ydr = ydram.rearrange("(a p) d -> a p d", p=128)
    ybf2 = [ybf_t, bview(ar1, 18 * KB, BF16, [128, 1024])]
    s_y = 0
    g_y = 0
    v_ytr = {0: 0, 1: 0}
    s_y_at = {}
    pend_y = None
    for tt in range(NT):
        par = tt % 2
        ybf = ybf2[par]
        W("sync", "s", s_y)
        vx = sc.dma("sync", xq_t[:], xqr[tt], "ldm")
        W("vector", "ldm", vx)
        W("vector", "s", s_y)
        v = sc.do("vector", lambda E, tt=tt: E.tensor_tensor(
            z1[:].rearrange("p (r d) -> p r d", r=16),
            xq_t[:].rearrange("p (r d) -> p r d", r=16),
            oat[:, tt, :].rearrange("p (r d) -> p r d", r=1).broadcast_to([128, 16, 64]),
            ALU.add), "v")
        sc.dep("vector", lambda E: E.reduce_sum(lnA[:], z1[:], axis=AXX))
        W("scalar", "v", sc.cnt["v"])
        W("scalar", "s", s_y)
        s_sq = sc.do("scalar", lambda E: E.activation(sqz[:], z1[:], AF.Square,
                                                      accum_out=lnB[:]), "s")
        W("vector", "s", s_sq)
        sc.dep("vector", lambda E: E.tensor_scalar(lnC[:], lnA[:], 1.0 / D, None, ALU.mult))
        sc.dep("vector", lambda E: E.tensor_scalar(lnD[:], lnB[:], 1.0 / D, None, ALU.mult))
        sc.dep("vector", lambda E: E.tensor_tensor(lnG[:], lnC[:], lnC[:], ALU.mult))
        v = sc.dep("vector", lambda E: E.tensor_tensor(lnD[:], lnD[:], lnG[:], ALU.subtract))
        W("scalar", "v", v)
        W("scalar", "s", s_sq)
        ss = sc.do("scalar", lambda E: E.activation(lnD[:], lnD[:], AF.Sqrt, bias=epsc[:, 0:1]), "s")
        W("vector", "s", ss)
        sc.dep("vector", lambda E: E.reciprocal(lnE_[:], lnD[:]))
        sc.dep("vector", lambda E: E.tensor_tensor(lnF[:], lnC[:], lnE_[:], ALU.mult))
        v_ln = sc.dep("vector", lambda E: E.tensor_scalar(lnF[:], lnF[:], -1.0, None, ALU.mult))
        W("scalar", "v", v_ln)
        W("scalar", "gd", g_y)
        W("scalar", "v", v_ytr[par])
        sc.do("scalar", lambda E, ybf=ybf: E.activation(ybf[:], z1[:], AF.Identity,
                                                        bias=lnF[:, 0:1], scale=lnE_[:, 0:1]), "s")
        s_y = sc.do("scalar", lambda E: E.activation(yf_t[:], z1[:], AF.Identity,
                                                     bias=lnF[:, 0:1], scale=lnE_[:, 0:1]), "s")
        s_y_at[tt] = s_y
        W("gpsimd", "s", s_y)
        g_y = sc.dma("gpsimd", ydr[tt], yf_t[:], "gd")
        if "y" in dbg:
            g_y = sc.dma("gpsimd", dbg["y"].rearrange("(a p) d -> a p d", p=128)[tt], yf_t[:], "gd")
        if pend_y is not None:
            ptt = pend_y
            W("tensor", "s", s_y_at[ptt])
            for e in range(8):
                tb = 6 + (e % 2)
                W("tensor", "v", v_ytr[e % 2])
                t = sc.do("tensor", lambda E, e=e, tb=tb, pybf=ybf2[ptt % 2]: E.transpose(
                    bankb(tb, cols=128), pybf[:, e * 128:(e + 1) * 128], ident_b), "t")
                W("vector", "t", t)
                v_ytr[e % 2] = sc.do("vector", lambda E, e=e, tt=ptt, tb=tb: E.tensor_copy(
                    yT[:, e, tt * 128:(tt + 1) * 128], bankb(tb, cols=128)), "v")
        pend_y = tt
    ptt = pend_y
    W("tensor", "s", s_y_at[ptt])
    for e in range(8):
        tb = 6 + (e % 2)
        W("tensor", "v", v_ytr[e % 2])
        t = sc.do("tensor", lambda E, e=e, tb=tb, pybf=ybf2[ptt % 2]: E.transpose(
            bankb(tb, cols=128), pybf[:, e * 128:(e + 1) * 128], ident_b), "t")
        W("vector", "t", t)
        v_ytr[e % 2] = sc.do("vector", lambda E, e=e, tt=ptt, tb=tb: E.tensor_copy(
            yT[:, e, tt * 128:(tt + 1) * 128], bankb(tb, cols=128)), "v")

    sc.barrier()
    if phases < 3:
        _finish(sc)
        return nc, dbg

    stgB = [bview(ar1, 0, F32, [128, 1024]), bview(ar1, 4 * KB, F32, [128, 1024])]
    Wsdb = bview(ar3, 13824 + 4096, BF16, [128, 8, 80])
    g5b2 = [bview(ar3, 19200, BF16, [128, 3, 320]), bview(ar3, 23040, BF16, [128, 3, 320])]
    for e in range(8):
        staged(Wsd[e * 128:(e + 1) * 128, :], Wsdb[:, e, :], e, stgB[e % 2][:, 0:80])
    W("tensor", "g", cast_m[e % 2])
    v_mask = 0
    s_sg = 0
    for tt in range(NT):
        W("tensor", "v", v_mask)
        W("tensor", "s", s_sg)
        for e in range(8):
            t = sc.do("tensor", lambda E, e=e, tt=tt: E.matmul(
                bank(0, cols=80), yT[:, e, tt * 128:(tt + 1) * 128], Wsdb[:, e, :],
                start=(e == 0), stop=(e == 7)), "t")
        W("vector", "t", t)
        v_sgc = sc.dep("vector", lambda E: E.tensor_copy(sgraw[:], bank(0, cols=80)))
        W("scalar", "t", t)
        W("scalar", "v", v_sgc)
        W("scalar", "s", s_sg)
        s_sg = sc.do("scalar", lambda E: E.activation(sgsig[:], bank(0, cols=80), AF.Sigmoid), "s")
        for half in (0, 1):
            raw = sgraw[:, half * 40: half * 40 + 40].rearrange("p (h e) -> p h e", h=8)
            v0, v1, v2, v3, v4 = (raw[:, :, i:i + 1] for i in range(5))
            m0, m1, m2, m3, m4, m5_ = (med[:, i, :].rearrange("p (h o) -> p h o", o=1)
                                       for i in range(6))
            for (o, a, b, op) in ((m0, v0, v1, ALU.min), (m1, v0, v1, ALU.max),
                                  (m2, v2, v3, ALU.min), (m3, v2, v3, ALU.max),
                                  (m4, m0, m2, ALU.max), (m5_, m1, m3, ALU.min),
                                  (m0, m4, m5_, ALU.min), (m1, m4, m5_, ALU.max),
                                  (m2, m1, v4, ALU.min), (m3, m0, m2, ALU.max)):
                sc.dep("vector", lambda E, o=o, a=a, b=b, op=op: E.tensor_tensor(o, a, b, op))
            medv = med[:, 3, :].rearrange("p (h o) -> p h o", o=1).broadcast_to([128, 8, 5])
            v_mask = sc.dep("vector", lambda E, raw=raw, medv=medv, half=half, tt=tt: E.tensor_tensor(
                (mdA if half else mdz)[:, tt, :].rearrange("p (h e) -> p h e", h=8),
                raw, medv, ALU.is_ge))
        W("vector", "s", s_sg)
        sc.dep("vector", lambda E, tt=tt: E.tensor_tensor(
            ssg[:, tt, :], sgsig[:, 0:40], mdz[:, tt, :], ALU.mult))
        sc.dep("vector", lambda E, tt=tt: E.tensor_tensor(
            sdg[:, tt, :], sgsig[:, 40:80], mdA[:, tt, :], ALU.mult))
        v_mask = sc.dep("vector", lambda E, tt=tt: E.tensor_scalar(
            sdg[:, tt, :], sdg[:, tt, :], 0.125, None, ALU.mult))
    if "gates" in dbg:
        W("gpsimd", "v", v_mask)
        sc.dma("gpsimd", dbg["gates"][:, 0: NT * 40], ssg[:].rearrange("p a b -> p (a b)"), "gd")
        sc.dma("gpsimd", dbg["gates"][:, NT * 40: NT * 80], sdg[:].rearrange("p a b -> p (a b)"), "gd")
    if "mdd" in dbg:
        W("gpsimd", "v", v_mask)
        sc.dma("gpsimd", dbg["mdd"][:, :], mdA[:].rearrange("p a b -> p (a b)"), "gd")

    qgh = bview(ar3, 13824, BF16, [128, 16, 64])
    kgh = bview(ar3, 13824 + 2048, BF16, [128, 16, 64])
    Wqh2 = [bview(ar1, 24 * KB, BF16, [128, 8, 320]), bview(ar1, 39 * KB, BF16, [128, 8, 320])]
    Wkh2 = [bview(ar1, 29 * KB, BF16, [128, 8, 320]), bview(ar1, 44 * KB, BF16, [128, 8, 320])]
    Wvh2 = [bview(ar1, 34 * KB, BF16, [128, 8, 320]), bview(ar1, 49 * KB, BF16, [128, 8, 320])]
    kTown = bview(ar1, 8 * KB, BF16, [128, 4, 2048])
    vga = bview(ar1, 64 * KB, BF16, [128, NT, H * 65])
    qTp = bview(ar1, 82 * KB, BF16, [128, 4, 2048])

    g_ones = sc.do("gpsimd", lambda E: E.memset(
        vga[:].rearrange("p a (h c) -> p a h c", c=65)[:, :, :, 64:65], 1.0), "g", 1)

    v_gate_guard = v_mask
    s_nrm = 0
    v_nrm = 0
    t_tr = 0
    t_h_done = {}
    ev_set = {0: 0, 1: 0}
    gate_done = {0: 0, 1: 0}
    first_nrm = True
    s_tr = 0
    W("tensor", "v", sc.cnt["v"])
    W("tensor", "s", sc.cnt["s"])
    for h in range(H):
        first_nrm = True
        hpar = h % 2
        Wqh, Wkh, Wvh = Wqh2[hpar], Wkh2[hpar], Wvh2[hpar]
        W("gpsimd", "t", t_h_done.get(h - 2, 0))
        for wi, (wsrc, wdst, c0) in enumerate(((Wqf, Wqh, h * 320), (Wkvf, Wkh, h * 320),
                                               (Wkvf, Wvh, 2560 + h * 320))):
            for e in range(8):
                staged(wsrc[e * 128:(e + 1) * 128, c0:c0 + 320], wdst[:, e, :],
                       wi * 8 + e, stgB[(wi * 8 + e) % 2][:, 0:320])
        v_w = max(cast_m[0], cast_m[1])
        W("tensor", "g", v_w)
        hp, hl = h // 2, h % 2
        pend_tr = None
        pend_nrm = None
        for tt in range(NT):
            par = tt % 2
            pbase = 0 if par == 0 else 3
            g5b = g5b2[par]
            W("tensor", "s", ev_set[par])
            for e in range(8):
                for (wdst, pb) in ((Wqh, 0), (Wkh, 1), (Wvh, 2)):
                    t = sc.do("tensor", lambda E, wdst=wdst, pb=pb + pbase, e=e, tt=tt: E.matmul(
                        bank(pb, cols=320), yT[:, e, tt * 128:(tt + 1) * 128], wdst[:, e, :],
                        start=(e == 0), stop=(e == 7)), "t")
            if pend_tr is not None:
                ptt = pend_tr
                W("tensor", "v", gate_done[1 - par])
                W("tensor", "s", s_tr)
                tq = sc.do("tensor", lambda E, ptt=ptt: E.transpose(
                    bankb(6, p1=64, cols=128), qgh[:, ptt, :], ident_b), "t")
                tk = sc.do("tensor", lambda E, ptt=ptt: E.transpose(
                    bankb(7, p1=64, cols=128), kgh[:, ptt, :], ident_b), "t")
                W("scalar", "t", tk)
                sc.do("scalar", lambda E, hp=hp, hl=hl, ptt=ptt: E.copy(
                    qTp[hl * 64:(hl + 1) * 64, hp, ptt * 128:(ptt + 1) * 128],
                    bankb(6, p1=64, cols=128)), "s")
                s_tr = sc.do("scalar", lambda E, hp=hp, hl=hl, ptt=ptt: E.copy(
                    kTown[hl * 64:(hl + 1) * 64, hp, ptt * 128:(ptt + 1) * 128],
                    bankb(7, p1=64, cols=128)), "s")
            pend_tr = tt
            if pend_nrm is not None:
                ntt, npar = pend_nrm, (1 - par)
                W("scalar", "v", gate_done[npar])
                sc.do("scalar", lambda E, ntt=ntt, npar=npar: E.activation(
                    sqscr[:], qgh[:, ntt, :], AF.Square, accum_out=nqt[:, npar:npar + 1]), "s")
                s_nrm = sc.do("scalar", lambda E, ntt=ntt, npar=npar, h=h: E.activation(
                    sqscr2[:], kgh[:, ntt, :], AF.Square, accum_out=nkt[:, npar:npar + 1]), "s")
                W("vector", "s", s_nrm)
                if first_nrm:
                    sc.dep("vector", lambda E, h=h, npar=npar: E.tensor_copy(nq[:, h:h + 1], nqt[:, npar:npar + 1]))
                    v_nrm = sc.dep("vector", lambda E, h=h, npar=npar: E.tensor_copy(nk[:, h:h + 1], nkt[:, npar:npar + 1]))
                    first_nrm = False
                else:
                    sc.dep("vector", lambda E, h=h, npar=npar: E.tensor_tensor(
                        nq[:, h:h + 1], nq[:, h:h + 1], nqt[:, npar:npar + 1], ALU.max))
                    v_nrm = sc.dep("vector", lambda E, h=h, npar=npar: E.tensor_tensor(
                        nk[:, h:h + 1], nk[:, h:h + 1], nkt[:, npar:npar + 1], ALU.max))
            W("scalar", "t", t)
            W("scalar", "v", gate_done[par])
            for j in range(3):
                ev = sc.do("scalar", lambda E, j=j, pbase=pbase, g5b=g5b: E.copy(
                    g5b[:, j, :], bank(pbase + j, cols=320)), "s")
            ev_set[par] = ev
            W("vector", "s", ev)
            for (j, gates, dst) in ((0, sdg, qgh[:, tt, :]), (1, ssg, kgh[:, tt, :]),
                                    (2, ssg, vga[:, tt, h * 65: h * 65 + 64])):
                sc.dep("vector", lambda E, j=j, gates=gates, dst=dst, tt=tt, h=h, g5b=g5b:
                       E.tensor_scalar(dst, g5b[:, j, 0:64],
                                       gates[:, tt, h * 5: h * 5 + 1], None, ALU.mult))
                for e5 in range(1, 5):
                    sc.dep("vector", lambda E, j=j, gates=gates, dst=dst, e5=e5, tt=tt, h=h, g5b=g5b:
                           E.scalar_tensor_tensor(dst, g5b[:, j, e5 * 64:(e5 + 1) * 64],
                                                  gates[:, tt, h * 5 + e5: h * 5 + e5 + 1],
                                                  dst, ALU.mult, ALU.add))
            gate_done[par] = sc.cnt["v"]
            pend_nrm = tt
        ptt = pend_tr
        W("tensor", "v", gate_done[ptt % 2])
        W("tensor", "s", s_tr)
        tq = sc.do("tensor", lambda E, ptt=ptt: E.transpose(
            bankb(6, p1=64, cols=128), qgh[:, ptt, :], ident_b), "t")
        tk = sc.do("tensor", lambda E, ptt=ptt: E.transpose(
            bankb(7, p1=64, cols=128), kgh[:, ptt, :], ident_b), "t")
        W("scalar", "t", tk)
        sc.do("scalar", lambda E, hp=hp, hl=hl, ptt=ptt: E.copy(
            qTp[hl * 64:(hl + 1) * 64, hp, ptt * 128:(ptt + 1) * 128],
            bankb(6, p1=64, cols=128)), "s")
        s_tr = sc.do("scalar", lambda E, hp=hp, hl=hl, ptt=ptt: E.copy(
            kTown[hl * 64:(hl + 1) * 64, hp, ptt * 128:(ptt + 1) * 128],
            bankb(7, p1=64, cols=128)), "s")
        ntt, npar = pend_nrm, pend_nrm % 2
        W("scalar", "v", gate_done[npar])
        sc.do("scalar", lambda E, ntt=ntt, npar=npar: E.activation(
            sqscr[:], qgh[:, ntt, :], AF.Square, accum_out=nqt[:, npar:npar + 1]), "s")
        s_nrm = sc.do("scalar", lambda E, ntt=ntt, npar=npar: E.activation(
            sqscr2[:], kgh[:, ntt, :], AF.Square, accum_out=nkt[:, npar:npar + 1]), "s")
        W("vector", "s", s_nrm)
        sc.dep("vector", lambda E, h=h, npar=npar: E.tensor_tensor(
            nq[:, h:h + 1], nq[:, h:h + 1], nqt[:, npar:npar + 1], ALU.max))
        v_nrm = sc.dep("vector", lambda E, h=h, npar=npar: E.tensor_tensor(
            nk[:, h:h + 1], nk[:, h:h + 1], nkt[:, npar:npar + 1], ALU.max))
        t_h_done[h] = sc.cnt["t"]

    W("tensor", "v", v_nrm)
    t = sc.do("tensor", lambda E: E.transpose(bank(0, p1=8, cols=128), nq[:], ident_f), "t")
    t2 = sc.do("tensor", lambda E: E.transpose(bank(1, p1=8, cols=128), nk[:], ident_f), "t")
    W("vector", "t", t2)
    sc.dep("vector", lambda E: E.tensor_copy(s8a[:], bank(0, p1=8, cols=128)))
    sc.dep("vector", lambda E: E.tensor_copy(s8b[:], bank(1, p1=8, cols=128)))
    sc.dep("vector", lambda E: E.reduce_max(r8a[:], s8a[:], axis=AXX))
    v = sc.dep("vector", lambda E: E.reduce_max(r8b[:], s8b[:], axis=AXX))
    v = sc.dep("vector", lambda E: E.tensor_tensor(r8a[:], r8a[:], r8b[:], ALU.mult))
    W("scalar", "v", v)
    s = sc.do("scalar", lambda E: E.activation(r8b[:], r8a[:], AF.Sqrt), "s")
    W("tensor", "s", s)
    t = sc.do("tensor", lambda E: E.transpose(bank(2, p1=1, cols=8), r8b[0:8, :], ident_f[0:8, 0:8]), "t")
    W("vector", "t", t)
    sc.dep("vector", lambda E: E.tensor_copy(row8[:], bank(2, p1=1, cols=8)))
    v = sc.dep("vector", lambda E: E.tensor_scalar(row8b[:], row8[:], -1.5, None, ALU.mult))
    v = sc.dep("vector", lambda E: E.tensor_copy(row8c[:], row8b[:]))
    W("tensor", "v", v)
    t = sc.do("tensor", lambda E: E.matmul(bank(2, cols=8), ones_b[:, :], row8c[:],
                                           start=True, stop=True), "t")
    W("vector", "t", t)
    v_mh = sc.dep("vector", lambda E: E.tensor_copy(mh[:], bank(2, cols=8)))
    if "mh" in dbg:
        W("gpsimd", "v", v_mh)
        sc.dma("gpsimd", dbg["mh"][:, :], mh[:], "gd")
    if "kTown" in dbg:
        W("gpsimd", "s", s_tr)
        sc.dma("gpsimd", dbg["kTown"][:, :], kTown[:].rearrange("p a b -> p (a b)"), "gd")
    if "vown" in dbg:
        W("gpsimd", "v", v_mh)
        W("gpsimd", "g", g_ones)
        sc.dma("gpsimd", dbg["vown"][:, :], vga[:].rearrange("p a b -> p (a b)"), "gd")

    W("gpsimd", "v", v_mh)
    W("gpsimd", "s", s_tr)
    W("gpsimd", "g", g_ones)
    ktr = ag_in[0:1, 0:KT_ELS].rearrange("o (hp p c) -> o hp p c", hp=4, p=128)
    g = sc.dma("gpsimd", ktr[0].rearrange("hp p c -> p hp c"), kTown[:], "gd")
    vtr = ag_in[0:1, KT_ELS:].rearrange("o (a p c) -> o a p c", a=NT, p=128)
    g = sc.dma("gpsimd", vtr[0].rearrange("a p c -> p a c"), vga[:], "gd")
    W("gpsimd", "gd", g)
    g_ag = sc.do("gpsimd", lambda E: E.collective_compute(
        "AllGather", ALU.bypass, replica_groups=GROUPS,
        ins=[ag_in.opt()], outs=[ag_out.opt()]), "cc", 1)

    sc.barrier()
    if phases < 4:
        _finish(sc)
        return nc, dbg

    accF = bview(ar1, 0, F32, [65, 16384])
    Ab = [[bview(ar2, int(48.25 * KB) + (2 * hl + pa) * 3 * KB, BF16, [128, 1536])
           for pa in range(2)] for hl in range(2)]
    vr = [bview(ar1, 64 * KB, BF16, [128, NT, H * 65]),
          bview(ar2, 32 * KB, BF16, [128, NT, H * 65])]
    kTr = [bview(ar2, 0, BF16, [128, 4, 2048]),
           bview(ar2, 16 * KB, BF16, [128, 4, 2048])]

    W("sync", "cc", g_ag)
    CHF = [(0, 3), (3, 3), (6, 3), (9, 3), (12, 3), (15, 1)]
    s_bank_last = {0: 0, 3: 0}
    v_accf = 0
    g_kv = {}
    t_r_done = {}
    for r in range(4):
        pr = r % 2
        if r == 0:
            for rr in (0, 1):
                ktro = ag_out[rr:rr + 1, 0:KT_ELS].rearrange("o (hp p c) -> o hp p c", hp=4, p=128)
                sc.dma("sync", kTr[rr % 2][:], ktro[0].rearrange("hp p c -> p hp c"), f"ldk{rr % 2}")
                vtro = ag_out[rr:rr + 1, KT_ELS:].rearrange("o (a p c) -> o a p c", a=NT, p=128)
                g_kv[rr] = sc.dma("sync", vr[rr % 2][:], vtro[0].rearrange("a p c -> p a c"), f"ldk{rr % 2}")
        elif r < 3:
            W("sync", "t", t_r_done[r - 1])
            rr = r + 1
            ktro = ag_out[rr:rr + 1, 0:KT_ELS].rearrange("o (hp p c) -> o hp p c", hp=4, p=128)
            sc.dma("sync", kTr[rr % 2][:], ktro[0].rearrange("hp p c -> p hp c"), f"ldk{rr % 2}")
            vtro = ag_out[rr:rr + 1, KT_ELS:].rearrange("o (a p c) -> o a p c", a=NT, p=128)
            g_kv[rr] = sc.dma("sync", vr[rr % 2][:], vtro[0].rearrange("a p c -> p a c"), f"ldk{rr % 2}")
        W("tensor", f"ldk{r % 2}", g_kv[r])
        if r == 0:
            W("scalar", "v", v_mh)
        for hp in range(4):
            for ib in range(4):
                s_exp_at = {}
                t_av_buf = {(0, 0): 0, (0, 1): 0, (1, 0): 0, (1, 1): 0}
                t_mm = {}
                bset = {}
                for ci, (j0, nj) in enumerate(CHF):
                    for hl in range(2):
                        b0 = 3 * hl
                        bset[(hl, ci)] = b0
                        W("tensor", "s", s_bank_last.get(b0, 0))
                        for jj in range(nj):
                            t = sc.do("tensor", lambda E, hl=hl, hp=hp, j0=j0, jj=jj, ib=ib, b0=b0, pr=pr: E.matmul(
                                bank(b0 + jj), kTr[pr][hl * 64:(hl + 1) * 64, hp, (j0 + jj) * 128:(j0 + jj + 1) * 128],
                                qTp[hl * 64:(hl + 1) * 64, hp, ib * 512:(ib + 1) * 512],
                                start=True, stop=True,
                                tile_position=(hl * 64, 0)), "t")
                        t_mm[hl] = t
                    for hl in range(2):
                        h = hp * 2 + hl
                        b0 = bset[(hl, ci)]
                        abuf = Ab[hl][ci % 2]
                        W("scalar", "t", t_mm[hl])
                        W("scalar", "t", t_av_buf[(hl, ci % 2)])
                        s_e = sc.do("scalar", lambda E, abuf=abuf, b0=b0, nj=nj, h=h: E.activation(
                            abuf[:, 0: nj * 512], bank(b0, cols=nj * 512), AF.Exp,
                            bias=mh[:, h:h + 1]), "s")
                        s_bank_last[b0] = s_e
                        s_exp_at[(hl, ci)] = s_e
                    for hl in range(2):
                        if ci == 0:
                            continue
                        _emit_av(sc, W, bank, vr, pr, qTp, accF, CHF, ci - 1, hl, hp,
                                 s_exp_at, t_av_buf, Ab, v_accf)
                for hl in range(2):
                    _emit_av(sc, W, bank, vr, pr, qTp, accF, CHF, 5, hl, hp,
                             s_exp_at, t_av_buf, Ab, v_accf)
                for hl in range(2):
                    h = hp * 2 + hl
                    W("vector", "t", t_av_buf[(hl, 5 % 2)])
                    col = (h * 4 + ib) * 512
                    if r == 0:
                        v_accf = sc.do("vector", lambda E, hl=hl, col=col: E.tensor_copy(
                            accF[:, col: col + 512], bank(6 + hl, p1=65)), "v")
                    else:
                        v_accf = sc.do("vector", lambda E, hl=hl, col=col: E.tensor_tensor(
                            accF[:, col: col + 512], bank(6 + hl, p1=65),
                            accF[:, col: col + 512], ALU.add), "v")
        t_r_done[r] = sc.cnt["t"]
    if "accF" in dbg:
        W("gpsimd", "v", v_accf)
        sc.dma("gpsimd", dbg["accF"][:, :], accF[:], "gd")

    sc.barrier()

    oaf = bview(ar2, 0, BF16, [128, NT, H * 64])
    zbuf = bview(ar2, 32 * KB, BF16, [128, 2560])
    zT = bview(ar2, 40 * KB, BF16, [128, 20, 128])
    WoBb = bview(ar1, 20 * KB, BF16, [128, 20, 1024])

    t_ep = 0
    v_ep = 0
    for h in range(H):
        for tt in range(NT):
            ib, sub = tt // 4, tt % 4
            col = (h * 4 + ib) * 512 + sub * 128
            W("tensor", "v", v_ep)
            t_ep = sc.do("tensor", lambda E, col=col: E.transpose(
                bank(0, p1=128, cols=65), accF[0:65, col:col + 128], ident_f[0:65, 0:65]), "t")
            W("vector", "t", t_ep)
            sc.do("vector", lambda E, h=h, tt=tt: E.tensor_copy(
                oaf[:, tt, h * 64:(h + 1) * 64], bank(0, cols=64)), "v")
            v_ep = sc.do("vector", lambda E, h=h, tt=tt: E.tensor_copy(
                ztok[:, tt, h:h + 1], bank(0, cols=1, off=64)), "v")
    v_rz2 = sc.dep("vector", lambda E: E.reciprocal(
        rztok[:].rearrange("p a b -> p (a b)"), ztok[:].rearrange("p a b -> p (a b)")))
    for tt in range(NT):
        v_mz = sc.dep("vector", lambda E, tt=tt: E.tensor_tensor(
            mdz[:, tt, :].rearrange("p (h e) -> p h e", h=8),
            mdA[:, tt, :].rearrange("p (h e) -> p h e", h=8),
            rztok[:, tt, :].rearrange("p (h e) -> p h e", e=1).broadcast_to([128, 8, 5]),
            ALU.mult))
    if "oaf" in dbg:
        W("gpsimd", "v", v_ep)
        sc.dma("gpsimd", dbg["oaf"][:, :], oaf[:].rearrange("p a b -> p (a b)"), "gd")
    if "rzt" in dbg:
        W("gpsimd", "v", v_rz2)
        sc.dma("gpsimd", dbg["rzt"][:, :], rztok[:].rearrange("p a b -> p (a b)"), "gd")

    W("sync", "v", max(v_ep, v_rz2))
    for kc in range(20):
        staged(WoB[kc * 128:(kc + 1) * 128, :], WoBb[:, kc, :], kc, stgB[kc % 2][:, 0:1024])
    v_wo = max(cast_m[0], cast_m[1])

    yf2 = bview(ar1, 60 * KB, F32, [128, 1024])
    res2 = bview(ar1, 64 * KB, F32, [128, 1024])
    sq2 = bview(ar1, 68 * KB, F32, [128, 1024])
    out_t = bview(ar1, 72 * KB, F32, [128, 1024])

    outr = out.rearrange("(a p) d -> a p d", p=128)
    zT2 = [zT, bview(ar2, 45 * KB, BF16, [128, 20, 128])]
    zbuf2 = [zbuf, bview(ar2, 50 * KB, BF16, [128, 2560])]
    s_o = 0
    g_out = 0
    v_res = 0
    v_zt = {0: 0, 1: 0}
    t_wo = {0: 0, 1: 0}
    for tt in range(NT):
        par = tt % 2
        zb, zt_ = zbuf2[par], zT2[par]
        W("vector", "v", v_mz)
        W("vector", "t", t_wo[par])
        v_z = sc.dep("vector", lambda E, tt=tt, zb=zb: E.tensor_tensor(
            zb[:].rearrange("p (h e d) -> p h e d", h=8, e=5),
            oaf[:, tt, :].rearrange("p (h o d) -> p h o d", h=8, o=1).broadcast_to([128, 8, 5, 64]),
            mdz[:, tt, :].rearrange("p (h e o) -> p h e o", h=8, o=1).broadcast_to([128, 8, 5, 64]),
            ALU.mult))
        for kc in range(20):
            tb = 6 + (kc % 2)
            W("tensor", "v", v_z if kc < 2 else v_zt[kc % 2])
            t = sc.do("tensor", lambda E, kc=kc, tb=tb, zb=zb: E.transpose(
                bankb(tb, cols=128), zb[:, kc * 128:(kc + 1) * 128], ident_b), "t")
            W("vector", "t", t)
            v_zt[kc % 2] = sc.do("vector", lambda E, kc=kc, tb=tb, zt_=zt_: E.tensor_copy(
                zt_[:, kc, :], bankb(tb, cols=128)), "v")
        v_zt_done = max(v_zt.values())
        W("tensor", "v", v_zt_done)
        W("tensor", "g", v_wo)
        W("tensor", "v", v_res)
        for mc in range(2):
            for kc in range(20):
                t = sc.do("tensor", lambda E, mc=mc, kc=kc, par=par, zt_=zt_: E.matmul(
                    bank(2 * par + mc, cols=512), zt_[:, kc, :], WoBb[:, kc, mc * 512:(mc + 1) * 512],
                    start=(kc == 0), stop=(kc == 19)), "t")
        t_wo[par] = t
        W("sync", "v", v_res)
        vx = sc.dma("sync", yf2[:], ydr[tt], "ldm")
        W("vector", "ldm", vx)
        W("vector", "t", t)
        W("vector", "s", s_o)
        for mc in range(2):
            v_res = sc.do("vector", lambda E, mc=mc, par=par: E.tensor_tensor(
                res2[:, mc * 512:(mc + 1) * 512], bank(2 * par + mc, cols=512),
                yf2[:, mc * 512:(mc + 1) * 512], ALU.add), "v")
        sc.dep("vector", lambda E: E.reduce_sum(lnA[:], res2[:], axis=AXX))
        W("scalar", "v", sc.cnt["v"])
        W("scalar", "s", s_o)
        s_sq = sc.do("scalar", lambda E: E.activation(sq2[:], res2[:], AF.Square,
                                                      accum_out=lnB[:]), "s")
        W("vector", "s", s_sq)
        sc.dep("vector", lambda E: E.tensor_scalar(lnC[:], lnA[:], 1.0 / D, None, ALU.mult))
        sc.dep("vector", lambda E: E.tensor_scalar(lnD[:], lnB[:], 1.0 / D, None, ALU.mult))
        sc.dep("vector", lambda E: E.tensor_tensor(lnG[:], lnC[:], lnC[:], ALU.mult))
        v = sc.dep("vector", lambda E: E.tensor_tensor(lnD[:], lnD[:], lnG[:], ALU.subtract))
        W("scalar", "v", v)
        W("scalar", "s", s_sq)
        ss = sc.do("scalar", lambda E: E.activation(lnD[:], lnD[:], AF.Sqrt, bias=epsc[:, 0:1]), "s")
        W("vector", "s", ss)
        sc.dep("vector", lambda E: E.reciprocal(lnE_[:], lnD[:]))
        sc.dep("vector", lambda E: E.tensor_tensor(lnF[:], lnC[:], lnE_[:], ALU.mult))
        v_ln = sc.dep("vector", lambda E: E.tensor_scalar(lnF[:], lnF[:], -1.0, None, ALU.mult))
        W("scalar", "v", v_ln)
        W("scalar", "gd", g_out)
        s_o = sc.do("scalar", lambda E: E.activation(out_t[:], res2[:], AF.Identity,
                                                     bias=lnF[:, 0:1], scale=lnE_[:, 0:1]), "s")
        W("gpsimd", "s", s_o)
        g_out = sc.dma("gpsimd", outr[tt], out_t[:], "gd")

    _finish(sc)
    return nc, dbg


def _finish(sc):
    sc.wait("gpsimd", "gd", sc.cnt.get("gd", 0))
    sc.replay()


def kernel(**inputs):
    x = inputs["L0_attn_x"]
    Wq, Wk, Wv = inputs["L0_attn_Wq"], inputs["L0_attn_Wk"], inputs["L0_attn_Wv"]
    Wqf, Wkvf = inputs["L0_ffn_Wq"], inputs["L0_ffn_Wkv"]
    Ws, Wd, Wo = inputs["L0_ffn_Ws"], inputs["L0_ffn_Wd"], inputs["L0_ffn_Wo"]

    WaT = np.ascontiguousarray(np.concatenate([Wq.T, Wk.T, Wv.T], axis=1)).astype(np.float32)
    Wsd = np.ascontiguousarray(np.concatenate([Ws, Wd], axis=1)).astype(np.float32)
    WoB = np.ascontiguousarray(
        np.tile(np.asarray(Wo).reshape(DH, E, D).transpose(1, 0, 2).reshape(E * DH, D), (H, 1)))

    nc, dbg = build()
    in_maps = []
    for c in range(8):
        b, q = c // 4, c % 4
        xb = np.asarray(x[b], dtype=np.float32)
        xTb = np.ascontiguousarray(xb.T)
        in_maps.append({
            "xT": xTb,
            "xqT": np.ascontiguousarray(xTb[:, q * SQ:(q + 1) * SQ]),
            "xq": np.ascontiguousarray(xb[q * SQ:(q + 1) * SQ]),
            "WaT": WaT, "Wsd": Wsd,
            "Wqf": np.ascontiguousarray(np.asarray(Wqf, dtype=np.float32)),
            "Wkvf": np.ascontiguousarray(np.asarray(Wkvf, dtype=np.float32)),
            "WoB": WoB.astype(np.float32),
        })
    res = run_bass_kernel_spmd(nc, in_maps, core_ids=list(range(8)))
    outp = np.empty((2, S, D), dtype=np.float32)
    for c in range(8):
        b, q = c // 4, c % 4
        outp[b, q * SQ:(q + 1) * SQ] = res.results[c]["out"]
    return outp


# version 12 (final)
# speedup vs baseline: 1.0791x; 1.0791x over previous
import numpy as np
import ml_dtypes
import concourse.bass as bass
import concourse.mybir as mybir
from concourse.bass_utils import run_bass_kernel_spmd

F32 = mybir.dt.float32
BF16 = mybir.dt.bfloat16
AF = mybir.ActivationFunctionType
ALU = mybir.AluOpType
AXX = mybir.AxisListType.X

S = 8192
SQ = 2048
D = 1024
DH = 64
H = 8
E = 5
NT = 16
GROUPS = [[0, 1, 2, 3], [4, 5, 6, 7]]
ENGS = ("sync", "gpsimd", "vector", "scalar", "tensor")
KT_ELS = 512 * SQ
V_ELS = SQ * (H * 65)
BLOB_ELS = KT_ELS + V_ELS
KB = 1024

BUILD_PHASES = 4


class Sched:
    def __init__(self, nc):
        self.nc = nc
        self.q = {e: [] for e in ENGS}
        self.sems = {}
        self.cnt = {}

    def sem(self, name):
        if name not in self.sems:
            self.sems[name] = self.nc.semaphore(name).__enter__()
            self.cnt[name] = 0
        return self.sems[name]

    def on(self, eng, fn):
        self.q[eng].append(fn)

    def wait(self, eng, sem, val):
        if not val:
            return
        h = self.sem(sem)
        self.on(eng, lambda E, h=h, v=val: E.wait_ge(h, v))

    def do(self, eng, fn, sem=None, inc=1):
        if sem is not None:
            h = self.sem(sem)
            self.cnt[sem] += inc
            v = self.cnt[sem]
            self.on(eng, lambda E, fn=fn, h=h, inc=inc: fn(E).then_inc(h, inc))
            return v
        self.on(eng, lambda E, fn=fn: fn(E))
        return None

    ENGSEM = {"vector": "v", "scalar": "s", "tensor": "t", "gpsimd": "g", "sync": "y"}

    def dep(self, eng, fn):
        m = self.ENGSEM[eng]
        self.wait(eng, m, self.cnt.get(m, 0))
        return self.do(eng, fn, m)

    def dma(self, eng, out, in_, sem="gd"):
        return self.do(eng, lambda E, o=out, i=in_: E.dma_start(out=o, in_=i), sem, 16)

    def barrier(self):
        self.wait("gpsimd", "gd", self.cnt.get("gd", 0))
        h = self.sem("bar")
        self.cnt["bar"] += len(ENGS)
        v = self.cnt["bar"]
        for e in ENGS:
            self.on(e, lambda E, h=h: E.nop().then_inc(h, 1))
            self.on(e, lambda E, h=h, v=v: E.wait_ge(h, v))

    def replay(self):
        with self.nc.Block() as block:
            @block.sync
            def _(E):
                for fn in self.q["sync"]:
                    fn(E)

            @block.gpsimd
            def _(E):
                for fn in self.q["gpsimd"]:
                    fn(E)

            @block.vector
            def _(E):
                for fn in self.q["vector"]:
                    fn(E)

            @block.scalar
            def _(E):
                for fn in self.q["scalar"]:
                    fn(E)

            @block.tensor
            def _(E):
                for fn in self.q["tensor"]:
                    fn(E)


def bview(arena, byte_off, dtype, shape):
    esz = mybir.dt.size(dtype)
    assert byte_off % esz == 0
    ap = arena[:].bitcast(dtype)
    parts, cols = shape[0], int(np.prod(shape[1:]))
    v = ap[0:parts, byte_off // esz: byte_off // esz + cols]
    if len(shape) > 2:
        names = " ".join(f"d{i}" for i in range(1, len(shape)))
        v = v.rearrange(f"p ({names}) -> p {names}",
                        **{f"d{i}": shape[i] for i in range(1, len(shape))})
    return v



def _emit_av(sc, W, bank, vr, pr, qTp, accF, CHF, ci, hl, hp, s_exp_at, t_av_buf, Ab, v_accf):
    j0, nj = CHF[ci]
    bav = 6 + hl
    h = hp * 2 + hl
    abuf = Ab[hl][ci % 2]
    W("tensor", "s", s_exp_at[(hl, ci)])
    if j0 == 0:
        W("tensor", "v", v_accf)
    ALU = mybir.AluOpType
    t_av = 0
    for jj in range(nj):
        t_av = sc.do("tensor", lambda E, bav=bav, abuf=abuf, jj=jj, j0=j0, h=h, pr=pr: E.matmul(
            bank(bav, p0=0, p1=65), vr[pr][:, j0 + jj, h * 65: h * 65 + 65],
            abuf[:, jj * 512:(jj + 1) * 512],
            start=(j0 == 0 and jj == 0), stop=(j0 + jj == 15)), "t")
    t_av_buf[(hl, ci % 2)] = t_av


def build(debug=(), phases=BUILD_PHASES, acut=99):
    nc = bass.Bass("TRN2", target_bir_lowering=False, debug=False)

    xT = nc.dram_tensor("xT", [D, S], F32, kind="ExternalInput").ap()
    xqT = nc.dram_tensor("xqT", [D, SQ], F32, kind="ExternalInput").ap()
    xq = nc.dram_tensor("xq", [SQ, D], F32, kind="ExternalInput").ap()
    WaT = nc.dram_tensor("WaT", [D, 192], F32, kind="ExternalInput").ap()
    Wsd = nc.dram_tensor("Wsd", [D, 80], F32, kind="ExternalInput").ap()
    Wqf = nc.dram_tensor("Wqf", [D, H * E * DH], F32, kind="ExternalInput").ap()
    Wkvf = nc.dram_tensor("Wkvf", [D, 2 * H * E * DH], F32, kind="ExternalInput").ap()
    WoB = nc.dram_tensor("WoB", [H * E * DH, D], F32, kind="ExternalInput").ap()
    out = nc.dram_tensor("out", [SQ, D], F32, kind="ExternalOutput").ap()

    dbg = {}
    def dbg_t(name, shape, dtype):
        if name in debug:
            dbg[name] = nc.dram_tensor("dbg_" + name, shape, dtype, kind="ExternalOutput").ap()

    dbg_t("acc", [128, 4096], F32)
    dbg_t("oat", [128, NT * DH], F32)
    dbg_t("y", [SQ, D], F32)
    dbg_t("gates", [128, NT * 80], F32)
    dbg_t("mdd", [128, NT * 40], F32)
    dbg_t("kTown", [128, 4 * SQ], BF16)
    dbg_t("vown", [128, NT * H * 65], BF16)
    dbg_t("accF", [65, 16384], F32)
    dbg_t("oaf", [128, NT * H * DH], BF16)
    dbg_t("rzt", [128, NT * H], F32)
    dbg_t("mh", [128, 8], F32)

    ydram = nc.dram_tensor("ydram", [SQ, D], F32).ap()
    rs_in = nc.dram_tensor("rs_in", [256, SQ], F32).ap()
    rs_out = nc.dram_tensor("rs_out", [64, SQ], F32).ap()
    ag_in = nc.dram_tensor("ag_in", [1, BLOB_ELS], BF16).ap()
    ag_out = nc.dram_tensor("ag_out", [4, BLOB_ELS], BF16).ap()

    ar1 = nc.sbuf_tensor("ar1", [128, 100 * KB], mybir.dt.uint8).__enter__()
    ar2 = nc.sbuf_tensor("ar2", [128, 64 * KB], mybir.dt.uint8).__enter__()
    ar3 = nc.sbuf_tensor("ar3", [128, 38 * KB], mybir.dt.uint8).__enter__()
    ps = nc.psum_tensor("ps", [128, 4096], F32).__enter__()
    psB = ps[:].bitcast(BF16)

    def bank(i, p0=0, p1=128, cols=512, off=0):
        return ps[p0:p1, i * 512 + off: i * 512 + off + cols]

    def bankb(i, p0=0, p1=128, cols=512, off=0):
        return psB[p0:p1, i * 1024 + off: i * 1024 + off + cols]

    sc = Sched(nc)
    W = sc.wait
    cast_m = {0: 0, 1: 0}

    def staged(src, dst, e, st_view):
        par = e % 2
        sem = f"ld{par}"
        W("sync", "g", cast_m[par])
        v = sc.dma("sync", st_view, src, sem)
        W("gpsimd", sem, v)
        cast_m[par] = sc.do("gpsimd", lambda E, dst=dst, st=st_view: E.tensor_copy(dst, st), "g")
        return cast_m[par]

    ident_f = bview(ar3, 32 * KB, F32, [128, 128])
    ident_b = bview(ar3, 32 * KB + 512, BF16, [128, 128])
    ones_b = bview(ar3, 32 * KB + 768, BF16, [1, 128])
    onec_b = bview(ar3, 32 * KB + 1024, BF16, [128, 1])
    sm = 34 * KB
    qmaxb = bview(ar3, sm, F32, [128, 1])
    epsc = bview(ar3, sm + 4000, F32, [128, 1])
    kn2 = bview(ar3, sm + 8, F32, [128, 16])
    knr = bview(ar3, sm + 72, F32, [128, 16])
    negmk = bview(ar3, sm + 136, F32, [128, 16])
    zp = bview(ar3, sm + 200, F32, [128, 8])
    zk = bview(ar3, sm + 232, F32, [128, 1])
    rzk = bview(ar3, sm + 236, F32, [128, 1])
    lnA = bview(ar3, sm + 240, F32, [128, 1])
    lnB = bview(ar3, sm + 244, F32, [128, 1])
    lnC = bview(ar3, sm + 248, F32, [128, 1])
    lnD = bview(ar3, sm + 252, F32, [128, 1])
    lnG = bview(ar3, sm + 256, F32, [128, 1])
    lnE_ = bview(ar3, sm + 260, F32, [128, 1])
    lnF = bview(ar3, sm + 264, F32, [128, 1])
    mh = bview(ar3, sm + 272, F32, [128, 8])
    nq = bview(ar3, sm + 304, F32, [128, 8])
    nk = bview(ar3, sm + 336, F32, [128, 8])
    nqt = bview(ar3, sm + 368, F32, [128, 2])
    nkt = bview(ar3, sm + 1000, F32, [128, 2])
    q1 = bview(ar3, sm + 376, F32, [1, 1])
    q2 = bview(ar3, sm + 380, F32, [1, 1])
    q3 = bview(ar3, sm + 384, BF16, [1, 1])
    qrow = bview(ar3, sm + 388, F32, [1, 4])
    s8a = bview(ar3, sm + 512, F32, [8, 128])
    s8b = bview(ar3, sm + 1536, F32, [8, 128])
    r8a = bview(ar3, sm + 2560, F32, [8, 1])
    r8b = bview(ar3, sm + 2564, F32, [8, 1])
    row8 = bview(ar3, sm + 2568, F32, [1, 8])
    row8b = bview(ar3, sm + 2600, F32, [1, 8])
    row8c = bview(ar3, sm + 2632, BF16, [1, 8])
    ztok = bview(ar3, sm + 2648, F32, [128, NT, 8])
    rztok = bview(ar3, sm + 3160, F32, [128, NT, 8])

    ssg = bview(ar3, 0, F32, [128, NT, 40])
    sdg = bview(ar3, 2560 * 1, F32, [128, NT, 40])
    mdA = bview(ar3, 2560 * 2, F32, [128, NT, 40])
    mdz = bview(ar3, 2560 * 3, F32, [128, NT, 40])
    sgraw = bview(ar3, 2560 * 4, F32, [128, 80])
    sgsig = bview(ar3, 2560 * 4 + 320, F32, [128, 80])
    med = bview(ar3, 2560 * 4 + 640, F32, [128, 6, 8])
    gaccq2 = [bview(ar3, 2560 * 4 + 832, F32, [128, 64]), bview(ar3, 12352, F32, [128, 64])]
    gacck2 = [bview(ar3, 2560 * 4 + 1088, F32, [128, 64]), bview(ar3, 12608, F32, [128, 64])]
    gaccv2 = [bview(ar3, 2560 * 4 + 1344, F32, [128, 64]), bview(ar3, 12864, F32, [128, 64])]
    sqscr = bview(ar3, 2560 * 4 + 1600, F32, [128, 64])
    sqscr2 = bview(ar3, 2560 * 4 + 1856, F32, [128, 64])

    v = sc.do("gpsimd", lambda E: E.memset(ident_f, 0.0), "g")
    v = sc.do("gpsimd", lambda E: E.memset(ident_b, 0.0), "g")
    v = sc.do("gpsimd", lambda E: E.memset(ones_b, 1.0), "g")
    v = sc.do("gpsimd", lambda E: E.memset(onec_b, 1.0), "g")
    v = sc.do("gpsimd", lambda E: E.memset(epsc, 1e-5), "g")
    W("gpsimd", "g", v)
    v = sc.do("gpsimd", lambda E: E.affine_select(
        out=ident_f, in_=ident_f, compare_op=ALU.not_equal,
        fill=1.0, base=0, pattern=[[-1, 128]], channel_multiplier=1), "g")
    g_setup = sc.do("gpsimd", lambda E: E.affine_select(
        out=ident_b, in_=ident_b, compare_op=ALU.not_equal,
        fill=1.0, base=0, pattern=[[-1, 128]], channel_multiplier=1), "g")
    W("gpsimd", "g", g_setup)

    stg = [bview(ar1, 0, F32, [128, 2048]), bview(ar1, 8 * KB, F32, [128, 2048])]
    xbbf = bview(ar1, 16 * KB, BF16, [128, 8, 2048])
    QT = bview(ar1, 48 * KB, BF16, [128, 8192])
    acc = bview(ar1, 64 * KB, F32, [128, 4096])
    Apr = bview(ar1, 80 * KB, BF16, [128, 8192])
    KT = bview(ar2, 0, BF16, [128, 2048])
    Ktok = bview(ar2, 4 * KB, BF16, [128, 16, 64])
    Vtok = bview(ar2, 6 * KB, BF16, [128, 16, 64])
    Vhat = bview(ar2, 8 * KB, BF16, [128, 64])
    WaTb = bview(ar2, 9 * KB, BF16, [128, 8, 192])
    sqb = bview(ar2, 13 * KB, BF16, [64, 2048])
    oat = bview(ar2, 18 * KB, F32, [128, NT, 64])
    arstage = bview(ar2, 23 * KB, F32, [64, 2048])

    if acut <= -1:
        _finish(sc)
        return nc, dbg
    for e in range(8):
        staged(WaT[e * 128:(e + 1) * 128, :], WaTb[:, e, :], e, stg[e % 2][:, 0:192])

    if acut <= 0:
        _finish(sc)
        return nc, dbg
    for e in range(8):
        staged(xqT[e * 128:(e + 1) * 128, :], xbbf[:, e, :], e, stg[e % 2])
    v_cast = cast_m[1]
    if acut <= 0.3:
        _finish(sc)
        return nc, dbg
    W("tensor", "g", v_cast)
    W("tensor", "g", g_setup)
    ev_bank = {}
    tr_ev = 0
    s_kn = 0
    for blk in range(16):
        pj = blk % 4
        W("tensor", "v", ev_bank.get(pj, 0))
        W("tensor", "s", s_kn if blk >= 4 else 0)
        for e in range(8):
            tmm = sc.do("tensor", lambda E, e=e, blk=blk, pj=pj: E.matmul(
                bank(pj, cols=128), xbbf[:, e, blk * 128:(blk + 1) * 128],
                WaTb[:, e, 64:192], start=(e == 0), stop=(e == 7)), "t")
        W("vector", "t", tmm)
        sc.do("vector", lambda E, pj=pj, blk=blk: E.tensor_copy(
            Ktok[:, blk, :], bank(pj, cols=64)), "v")
        ev = sc.do("vector", lambda E, pj=pj, blk=blk: E.tensor_copy(
            Vtok[:, blk, :], bank(pj, cols=64, off=64)), "v")
        ev_bank[pj] = ev
        if acut > 0.8:
            W("tensor", "v", ev)
            t = sc.do("tensor", lambda E, blk=blk: E.transpose(
                bankb(5, p1=64, cols=128), Ktok[:, blk, :], ident_b), "t")
            W("vector", "t", t)
            tr_ev = sc.do("vector", lambda E, blk=blk: E.tensor_copy(
                KT[0:64, blk * 128: blk * 128 + 128], bankb(5, p1=64, cols=128)), "v")
            W("tensor", "v", tr_ev)
    s_kn_done = s_kn
    if acut <= 1:
        _finish(sc)
        return nc, dbg
    W("tensor", "s", s_kn_done)

    t_qband = sc.cnt["t"]
    v_qev = 0
    s_sq = 0
    t_qn = 0
    v_qn = 0
    qrow_init = True
    for band in range(4):
        for e in range(8):
            if e == 0:
                W("gpsimd", "t", t_qband)
            staged(xT[e * 128:(e + 1) * 128, band * 2048:(band + 1) * 2048],
                   xbbf[:, e, :], e, stg[e % 2])
        W("tensor", "g", cast_m[e % 2])
        W("tensor", "v", v_qev)
        W("tensor", "s", s_sq)
        for sub in range(4):
            for e in range(8):
                t_qband = sc.do("tensor", lambda E, sub=sub, e=e: E.matmul(
                    bank(sub, p1=64), WaTb[:, e, 0:64], xbbf[:, e, sub * 512:(sub + 1) * 512],
                    start=(e == 0), stop=(e == 7)), "t")
        W("vector", "t", t_qband)
        for sub in range(4):
            v_qev = sc.do("vector", lambda E, band=band, sub=sub: E.tensor_copy(
                QT[0:64, band * 2048 + sub * 512: band * 2048 + (sub + 1) * 512],
                bank(sub, p1=64)), "v")
        s_sq = s_sq
    if acut <= 2:
        _finish(sc)
        return nc, dbg
    if acut <= 3:
        _finish(sc)
        return nc, dbg
    CH = [(0, 1536), (1536, 1536), (3072, 1536), (4608, 1536), (6144, 1536), (7680, 512)]
    s_exp_set = {0: 0, 3: 0}
    v_rmx_set = {0: 0, 3: 0}
    W("tensor", "v", sc.cnt["v"])
    av_evac = sc.cnt["v"]
    prev_avv = {6: 0, 7: 0}
    rmxp = zp
    for kblk in range(16):
        for ci, (ibase, fd) in enumerate(CH):
            b0 = 0 if ci % 2 == 0 else 3
            W("tensor", "s", s_exp_set[b0])
            W("tensor", "v", v_rmx_set[b0])
            for sub in range(fd // 512):
                t = sc.do("tensor", lambda E, kblk=kblk, b0=b0, sub=sub, ibase=ibase: E.matmul(
                    bank(b0 + sub), KT[0:64, kblk * 128:(kblk + 1) * 128],
                    QT[0:64, ibase + sub * 512: ibase + (sub + 1) * 512],
                    start=True, stop=True), "t")
            W("vector", "t", t)
            v_rmx_set[b0] = sc.dep("vector", lambda E, b0=b0, fd=fd, ci=ci: E.reduce_max(
                knr[:, ci: ci + 1], bank(b0, cols=fd), axis=AXX))
        v = sc.dep("vector", lambda E: E.reduce_max(rzk[:], knr[:, 0:6], axis=AXX))
        v_negm = sc.dep("vector", lambda E, kblk=kblk: E.tensor_scalar(
            negmk[:, kblk: kblk + 1], rzk[:], -1.0, None, ALU.mult))
        W("scalar", "v", v_negm)
        for ci, (ibase, fd) in enumerate(CH):
            b0 = 0 if ci % 2 == 0 else 3
            cidx = kblk * 6 + ci
            if ci >= 2:
                W("tensor", "s", s_exp_set[b0])
            else:
                W("tensor", "v", av_evac)
            W("tensor", "v", v_rmx_set[b0])
            for sub in range(fd // 512):
                t = sc.do("tensor", lambda E, kblk=kblk, b0=b0, sub=sub, ibase=ibase: E.matmul(
                    bank(b0 + sub), KT[0:64, kblk * 128:(kblk + 1) * 128],
                    QT[0:64, ibase + sub * 512: ibase + (sub + 1) * 512],
                    start=True, stop=True), "t")
            W("scalar", "t", t)
            s_exp_set[b0] = sc.do("scalar", lambda E, b0=b0, fd=fd, ibase=ibase, ci=ci, kblk=kblk: E.activation(
                Apr[:, ibase: ibase + fd], bank(b0, cols=fd), AF.Exp,
                bias=negmk[:, kblk: kblk + 1], accum_out=zp[:, ci: ci + 1]), "s")
        W("vector", "s", s_exp_set[b0])
        sc.dep("vector", lambda E: E.reduce_sum(zk[:], zp[:, 0:6], axis=AXX))
        v_rz = sc.dep("vector", lambda E: E.reciprocal(rzk[:], zk[:]))
        W("scalar", "v", v_rz)
        s_vh = sc.do("scalar", lambda E, kblk=kblk: E.mul(Vhat[:], Vtok[:, kblk, :], rzk[:, 0:1]), "s")
        W("tensor", "s", s_vh)
        for pr in range(8):
            bav = 6 + (pr % 2)
            W("tensor", "v", prev_avv[bav])
            t1 = sc.do("tensor", lambda E, bav=bav, pr=pr: E.matmul(
                bank(bav, p0=0, p1=64), Vhat[:], Apr[:, (2 * pr) * 512:(2 * pr + 1) * 512],
                start=True, stop=True, tile_position=(0, 0)), "t")
            t2 = sc.do("tensor", lambda E, bav=bav, pr=pr: E.matmul(
                bank(bav, p0=64, p1=128), Vhat[:], Apr[:, (2 * pr + 1) * 512:(2 * pr + 2) * 512],
                start=True, stop=True, tile_position=(0, 64)), "t")
            W("vector", "t", t2)
            if kblk == 0:
                ev = sc.do("vector", lambda E, bav=bav, pr=pr: E.tensor_copy(
                    acc[:, pr * 512:(pr + 1) * 512], bank(bav)), "v")
            else:
                ev = sc.do("vector", lambda E, bav=bav, pr=pr: E.tensor_tensor(
                    acc[:, pr * 512:(pr + 1) * 512], bank(bav),
                    acc[:, pr * 512:(pr + 1) * 512], ALU.add), "v")
            prev_avv[bav] = ev
        av_evac = sc.cnt["v"]

    if acut <= 4:
        _finish(sc)
        return nc, dbg
    if "acc" in dbg:
        sc.wait("gpsimd", "v", av_evac)
        sc.dma("gpsimd", dbg["acc"][:, :], acc[:], "gd")

    W("gpsimd", "v", av_evac)
    for qr in range(4):
        for prl in range(2):
            for par in range(2):
                g = sc.dma("gpsimd",
                           rs_in[qr * 64:(qr + 1) * 64, prl * 1024 + par * 512: prl * 1024 + (par + 1) * 512],
                           acc[par * 64:(par + 1) * 64, (2 * qr + prl) * 512:(2 * qr + prl + 1) * 512],
                           "gd")
    W("gpsimd", "gd", g)
    g_cc = sc.do("gpsimd", lambda E: E.collective_compute(
        "ReduceScatter", ALU.add, replica_groups=GROUPS,
        ins=[rs_in.opt()], outs=[rs_out.opt()]), "cc", 1)

    if acut <= 5:
        _finish(sc)
        return nc, dbg
    W("sync", "cc", g_cc)
    v = sc.dma("sync", arstage[:, :], rs_out[:, :], "ldm")
    W("tensor", "ldm", v)
    for tt in range(NT):
        t = sc.do("tensor", lambda E, tt=tt: E.transpose(
            bank(0, cols=64, off=(tt % 4) * 64), arstage[0:64, tt * 128:(tt + 1) * 128],
            ident_f[0:64, 0:64]), "t")
        if tt % 4 == 3:
            W("vector", "t", t)
            for u in range(4):
                v = sc.do("vector", lambda E, tt=tt, u=u: E.tensor_copy(
                    oat[:, tt - 3 + u, :], bank(0, cols=64, off=u * 64)), "v")
            W("tensor", "v", v)
    if "oat" in dbg:
        W("gpsimd", "v", sc.cnt["v"])
        sc.dma("gpsimd", dbg["oat"][:, :], oat[:].rearrange("p a b -> p (a b)"), "gd")

    sc.barrier()
    if phases < 2:
        _finish(sc)
        return nc, dbg

    xq_t = bview(ar1, 0, F32, [128, 1024])
    z1 = bview(ar1, 4 * KB, F32, [128, 1024])
    sqz = bview(ar1, 8 * KB, F32, [128, 1024])
    ybf_t = bview(ar1, 12 * KB, BF16, [128, 1024])
    yf_t = bview(ar1, 14 * KB, F32, [128, 1024])
    yT = bview(ar2, 32 * KB, BF16, [128, 8, 2048])

    xqr = xq.rearrange("(a p) d -> a p d", p=128)
    ydr = ydram.rearrange("(a p) d -> a p d", p=128)
    ybf2 = [ybf_t, bview(ar1, 18 * KB, BF16, [128, 1024])]
    s_y = 0
    g_y = 0
    v_ytr = {0: 0, 1: 0}
    s_y_at = {}
    pend_y = None
    for tt in range(NT):
        par = tt % 2
        ybf = ybf2[par]
        W("sync", "s", s_y)
        vx = sc.dma("sync", xq_t[:], xqr[tt], "ldm")
        W("vector", "ldm", vx)
        W("vector", "s", s_y)
        v = sc.do("vector", lambda E, tt=tt: E.tensor_tensor(
            z1[:].rearrange("p (r d) -> p r d", r=16),
            xq_t[:].rearrange("p (r d) -> p r d", r=16),
            oat[:, tt, :].rearrange("p (r d) -> p r d", r=1).broadcast_to([128, 16, 64]),
            ALU.add), "v")
        sc.dep("vector", lambda E: E.reduce_sum(lnA[:], z1[:], axis=AXX))
        W("scalar", "v", sc.cnt["v"])
        W("scalar", "s", s_y)
        s_sq = sc.do("scalar", lambda E: E.activation(sqz[:], z1[:], AF.Square,
                                                      accum_out=lnB[:]), "s")
        W("vector", "s", s_sq)
        sc.dep("vector", lambda E: E.tensor_scalar(lnC[:], lnA[:], 1.0 / D, None, ALU.mult))
        sc.dep("vector", lambda E: E.tensor_scalar(lnD[:], lnB[:], 1.0 / D, None, ALU.mult))
        sc.dep("vector", lambda E: E.tensor_tensor(lnG[:], lnC[:], lnC[:], ALU.mult))
        v = sc.dep("vector", lambda E: E.tensor_tensor(lnD[:], lnD[:], lnG[:], ALU.subtract))
        W("scalar", "v", v)
        W("scalar", "s", s_sq)
        ss = sc.do("scalar", lambda E: E.activation(lnD[:], lnD[:], AF.Sqrt, bias=epsc[:, 0:1]), "s")
        W("vector", "s", ss)
        sc.dep("vector", lambda E: E.reciprocal(lnE_[:], lnD[:]))
        sc.dep("vector", lambda E: E.tensor_tensor(lnF[:], lnC[:], lnE_[:], ALU.mult))
        v_ln = sc.dep("vector", lambda E: E.tensor_scalar(lnF[:], lnF[:], -1.0, None, ALU.mult))
        W("scalar", "v", v_ln)
        W("scalar", "gd", g_y)
        W("scalar", "v", v_ytr[par])
        sc.do("scalar", lambda E, ybf=ybf: E.activation(ybf[:], z1[:], AF.Identity,
                                                        bias=lnF[:, 0:1], scale=lnE_[:, 0:1]), "s")
        s_y = sc.do("scalar", lambda E: E.activation(yf_t[:], z1[:], AF.Identity,
                                                     bias=lnF[:, 0:1], scale=lnE_[:, 0:1]), "s")
        s_y_at[tt] = s_y
        W("gpsimd", "s", s_y)
        g_y = sc.dma("gpsimd", ydr[tt], yf_t[:], "gd")
        if "y" in dbg:
            g_y = sc.dma("gpsimd", dbg["y"].rearrange("(a p) d -> a p d", p=128)[tt], yf_t[:], "gd")
        if pend_y is not None:
            ptt = pend_y
            W("tensor", "s", s_y_at[ptt])
            for e in range(8):
                tb = 6 + (e % 2)
                W("tensor", "v", v_ytr[e % 2])
                t = sc.do("tensor", lambda E, e=e, tb=tb, pybf=ybf2[ptt % 2]: E.transpose(
                    bankb(tb, cols=128), pybf[:, e * 128:(e + 1) * 128], ident_b), "t")
                W("vector", "t", t)
                v_ytr[e % 2] = sc.do("vector", lambda E, e=e, tt=ptt, tb=tb: E.tensor_copy(
                    yT[:, e, tt * 128:(tt + 1) * 128], bankb(tb, cols=128)), "v")
        pend_y = tt
    ptt = pend_y
    W("tensor", "s", s_y_at[ptt])
    for e in range(8):
        tb = 6 + (e % 2)
        W("tensor", "v", v_ytr[e % 2])
        t = sc.do("tensor", lambda E, e=e, tb=tb, pybf=ybf2[ptt % 2]: E.transpose(
            bankb(tb, cols=128), pybf[:, e * 128:(e + 1) * 128], ident_b), "t")
        W("vector", "t", t)
        v_ytr[e % 2] = sc.do("vector", lambda E, e=e, tt=ptt, tb=tb: E.tensor_copy(
            yT[:, e, tt * 128:(tt + 1) * 128], bankb(tb, cols=128)), "v")

    sc.barrier()
    if phases < 3:
        _finish(sc)
        return nc, dbg

    stgB = [bview(ar1, 0, F32, [128, 1024]), bview(ar1, 4 * KB, F32, [128, 1024])]
    Wsdb = bview(ar3, 13824 + 4096, BF16, [128, 8, 80])
    gtmp = bview(ar3, 19200, F32, [128, 5, 64])
    for e in range(8):
        staged(Wsd[e * 128:(e + 1) * 128, :], Wsdb[:, e, :], e, stgB[e % 2][:, 0:80])
    W("tensor", "g", cast_m[e % 2])
    v_mask = 0
    s_sg = 0
    for tt in range(NT):
        W("tensor", "v", v_mask)
        W("tensor", "s", s_sg)
        for e in range(8):
            t = sc.do("tensor", lambda E, e=e, tt=tt: E.matmul(
                bank(0, cols=80), yT[:, e, tt * 128:(tt + 1) * 128], Wsdb[:, e, :],
                start=(e == 0), stop=(e == 7)), "t")
        W("vector", "t", t)
        v_sgc = sc.dep("vector", lambda E: E.tensor_copy(sgraw[:], bank(0, cols=80)))
        W("scalar", "t", t)
        W("scalar", "v", v_sgc)
        W("scalar", "s", s_sg)
        s_sg = sc.do("scalar", lambda E: E.activation(sgsig[:], bank(0, cols=80), AF.Sigmoid), "s")
        for half in (0, 1):
            raw = sgraw[:, half * 40: half * 40 + 40].rearrange("p (h e) -> p h e", h=8)
            v0, v1, v2, v3, v4 = (raw[:, :, i:i + 1] for i in range(5))
            m0, m1, m2, m3, m4, m5_ = (med[:, i, :].rearrange("p (h o) -> p h o", o=1)
                                       for i in range(6))
            for (o, a, b, op) in ((m0, v0, v1, ALU.min), (m1, v0, v1, ALU.max),
                                  (m2, v2, v3, ALU.min), (m3, v2, v3, ALU.max),
                                  (m4, m0, m2, ALU.max), (m5_, m1, m3, ALU.min),
                                  (m0, m4, m5_, ALU.min), (m1, m4, m5_, ALU.max),
                                  (m2, m1, v4, ALU.min), (m3, m0, m2, ALU.max)):
                sc.dep("vector", lambda E, o=o, a=a, b=b, op=op: E.tensor_tensor(o, a, b, op))
            medv = med[:, 3, :].rearrange("p (h o) -> p h o", o=1).broadcast_to([128, 8, 5])
            v_mask = sc.dep("vector", lambda E, raw=raw, medv=medv, half=half, tt=tt: E.tensor_tensor(
                (mdA if half else mdz)[:, tt, :].rearrange("p (h e) -> p h e", h=8),
                raw, medv, ALU.is_ge))
        W("vector", "s", s_sg)
        sc.dep("vector", lambda E, tt=tt: E.tensor_tensor(
            ssg[:, tt, :], sgsig[:, 0:40], mdz[:, tt, :], ALU.mult))
        sc.dep("vector", lambda E, tt=tt: E.tensor_tensor(
            sdg[:, tt, :], sgsig[:, 40:80], mdA[:, tt, :], ALU.mult))
        v_mask = sc.dep("vector", lambda E, tt=tt: E.tensor_scalar(
            sdg[:, tt, :], sdg[:, tt, :], 0.125, None, ALU.mult))
    if "gates" in dbg:
        W("gpsimd", "v", v_mask)
        sc.dma("gpsimd", dbg["gates"][:, 0: NT * 40], ssg[:].rearrange("p a b -> p (a b)"), "gd")
        sc.dma("gpsimd", dbg["gates"][:, NT * 40: NT * 80], sdg[:].rearrange("p a b -> p (a b)"), "gd")
    if "mdd" in dbg:
        W("gpsimd", "v", v_mask)
        sc.dma("gpsimd", dbg["mdd"][:, :], mdA[:].rearrange("p a b -> p (a b)"), "gd")

    qgh = bview(ar3, 13824, BF16, [128, 16, 64])
    kgh = bview(ar3, 13824 + 2048, BF16, [128, 16, 64])
    Wqh2 = [bview(ar1, 24 * KB, BF16, [128, 8, 320]), bview(ar1, 39 * KB, BF16, [128, 8, 320])]
    Wkh2 = [bview(ar1, 29 * KB, BF16, [128, 8, 320]), bview(ar1, 44 * KB, BF16, [128, 8, 320])]
    Wvh2 = [bview(ar1, 34 * KB, BF16, [128, 8, 320]), bview(ar1, 49 * KB, BF16, [128, 8, 320])]
    kTown = bview(ar1, 8 * KB, BF16, [128, 4, 2048])
    vga = bview(ar1, 64 * KB, BF16, [128, NT, H * 65])
    qTp = bview(ar1, 82 * KB, BF16, [128, 4, 2048])

    g_ones = sc.do("gpsimd", lambda E: E.memset(
        vga[:].rearrange("p a (h c) -> p a h c", c=65)[:, :, :, 64:65], 1.0), "g", 1)

    v_gate_guard = v_mask
    s_nrm = 0
    v_nrm = 0
    t_tr = 0
    t_h_done = {}
    ev_set = {0: 0, 1: 0}
    gate_done = {0: 0, 1: 0}
    first_nrm = True
    s_tr = 0
    W("tensor", "v", sc.cnt["v"])
    W("tensor", "s", sc.cnt["s"])
    for h in range(H):
        first_nrm = True
        hpar = h % 2
        Wqh, Wkh, Wvh = Wqh2[hpar], Wkh2[hpar], Wvh2[hpar]
        W("gpsimd", "t", t_h_done.get(h - 2, 0))
        for wi, (wsrc, wdst, c0) in enumerate(((Wqf, Wqh, h * 320), (Wkvf, Wkh, h * 320),
                                               (Wkvf, Wvh, 2560 + h * 320))):
            for e in range(8):
                staged(wsrc[e * 128:(e + 1) * 128, c0:c0 + 320], wdst[:, e, :],
                       wi * 8 + e, stgB[(wi * 8 + e) % 2][:, 0:320])
        v_w = max(cast_m[0], cast_m[1])
        W("tensor", "g", v_w)
        hp, hl = h // 2, h % 2
        pend_tr = None
        pend_nrm = None
        for tt in range(NT):
            par = tt % 2
            pbase = 0 if par == 0 else 3
            W("tensor", "v", ev_set[par])
            for e in range(8):
                for (wdst, pb) in ((Wqh, 0), (Wkh, 1), (Wvh, 2)):
                    t = sc.do("tensor", lambda E, wdst=wdst, pb=pb + pbase, e=e, tt=tt: E.matmul(
                        bank(pb, cols=320), yT[:, e, tt * 128:(tt + 1) * 128], wdst[:, e, :],
                        start=(e == 0), stop=(e == 7)), "t")
            if pend_tr is not None:
                ptt = pend_tr
                W("tensor", "v", gate_done[1 - par])
                W("tensor", "s", s_tr)
                tq = sc.do("tensor", lambda E, ptt=ptt: E.transpose(
                    bankb(6, p1=64, cols=128), qgh[:, ptt, :], ident_b), "t")
                tk = sc.do("tensor", lambda E, ptt=ptt: E.transpose(
                    bankb(7, p1=64, cols=128), kgh[:, ptt, :], ident_b), "t")
                W("scalar", "t", tk)
                sc.do("scalar", lambda E, hp=hp, hl=hl, ptt=ptt: E.copy(
                    qTp[hl * 64:(hl + 1) * 64, hp, ptt * 128:(ptt + 1) * 128],
                    bankb(6, p1=64, cols=128)), "s")
                s_tr = sc.do("scalar", lambda E, hp=hp, hl=hl, ptt=ptt: E.copy(
                    kTown[hl * 64:(hl + 1) * 64, hp, ptt * 128:(ptt + 1) * 128],
                    bankb(7, p1=64, cols=128)), "s")
            pend_tr = tt
            if pend_nrm is not None:
                ntt, npar = pend_nrm, (1 - par)
                W("scalar", "v", gate_done[npar])
                sc.do("scalar", lambda E, ntt=ntt, npar=npar: E.activation(
                    sqscr[:], qgh[:, ntt, :], AF.Square, accum_out=nqt[:, npar:npar + 1]), "s")
                s_nrm = sc.do("scalar", lambda E, ntt=ntt, npar=npar, h=h: E.activation(
                    sqscr2[:], kgh[:, ntt, :], AF.Square, accum_out=nkt[:, npar:npar + 1]), "s")
                W("vector", "s", s_nrm)
                if first_nrm:
                    sc.dep("vector", lambda E, h=h, npar=npar: E.tensor_copy(nq[:, h:h + 1], nqt[:, npar:npar + 1]))
                    v_nrm = sc.dep("vector", lambda E, h=h, npar=npar: E.tensor_copy(nk[:, h:h + 1], nkt[:, npar:npar + 1]))
                    first_nrm = False
                else:
                    sc.dep("vector", lambda E, h=h, npar=npar: E.tensor_tensor(
                        nq[:, h:h + 1], nq[:, h:h + 1], nqt[:, npar:npar + 1], ALU.max))
                    v_nrm = sc.dep("vector", lambda E, h=h, npar=npar: E.tensor_tensor(
                        nk[:, h:h + 1], nk[:, h:h + 1], nkt[:, npar:npar + 1], ALU.max))
            W("vector", "t", t)
            for (j, gates, dst) in ((0, sdg, qgh[:, tt, :]), (1, ssg, kgh[:, tt, :]),
                                    (2, ssg, vga[:, tt, h * 65: h * 65 + 64])):
                sc.dep("vector", lambda E, j=j, gates=gates, tt=tt, h=h, pbase=pbase:
                       E.tensor_tensor(
                           gtmp[:],
                           bank(pbase + j, cols=320).rearrange("p (e d) -> p e d", e=5),
                           gates[:, tt, h * 5: h * 5 + 5].rearrange("p (e o) -> p e o", o=1)
                           .broadcast_to([128, 5, 64]),
                           ALU.mult))
                def _red(E, dst=dst):
                    with E.bass.allow_low_precision(reason="5-term gate sum; fp32 internal"):
                        return E.reduce_sum(dst.rearrange("p (d o) -> p d o", o=1),
                                            gtmp[:].rearrange("p e d -> p d e"), axis=AXX)
                ev = sc.dep("vector", _red)
            ev_set[par] = sc.cnt["v"] - 1
            gate_done[par] = sc.cnt["v"]
            pend_nrm = tt
        ptt = pend_tr
        W("tensor", "v", gate_done[ptt % 2])
        W("tensor", "s", s_tr)
        tq = sc.do("tensor", lambda E, ptt=ptt: E.transpose(
            bankb(6, p1=64, cols=128), qgh[:, ptt, :], ident_b), "t")
        tk = sc.do("tensor", lambda E, ptt=ptt: E.transpose(
            bankb(7, p1=64, cols=128), kgh[:, ptt, :], ident_b), "t")
        W("scalar", "t", tk)
        sc.do("scalar", lambda E, hp=hp, hl=hl, ptt=ptt: E.copy(
            qTp[hl * 64:(hl + 1) * 64, hp, ptt * 128:(ptt + 1) * 128],
            bankb(6, p1=64, cols=128)), "s")
        s_tr = sc.do("scalar", lambda E, hp=hp, hl=hl, ptt=ptt: E.copy(
            kTown[hl * 64:(hl + 1) * 64, hp, ptt * 128:(ptt + 1) * 128],
            bankb(7, p1=64, cols=128)), "s")
        ntt, npar = pend_nrm, pend_nrm % 2
        W("scalar", "v", gate_done[npar])
        sc.do("scalar", lambda E, ntt=ntt, npar=npar: E.activation(
            sqscr[:], qgh[:, ntt, :], AF.Square, accum_out=nqt[:, npar:npar + 1]), "s")
        s_nrm = sc.do("scalar", lambda E, ntt=ntt, npar=npar: E.activation(
            sqscr2[:], kgh[:, ntt, :], AF.Square, accum_out=nkt[:, npar:npar + 1]), "s")
        W("vector", "s", s_nrm)
        sc.dep("vector", lambda E, h=h, npar=npar: E.tensor_tensor(
            nq[:, h:h + 1], nq[:, h:h + 1], nqt[:, npar:npar + 1], ALU.max))
        v_nrm = sc.dep("vector", lambda E, h=h, npar=npar: E.tensor_tensor(
            nk[:, h:h + 1], nk[:, h:h + 1], nkt[:, npar:npar + 1], ALU.max))
        t_h_done[h] = sc.cnt["t"]

    W("tensor", "v", v_nrm)
    t = sc.do("tensor", lambda E: E.transpose(bank(0, p1=8, cols=128), nq[:], ident_f), "t")
    t2 = sc.do("tensor", lambda E: E.transpose(bank(1, p1=8, cols=128), nk[:], ident_f), "t")
    W("vector", "t", t2)
    sc.dep("vector", lambda E: E.tensor_copy(s8a[:], bank(0, p1=8, cols=128)))
    sc.dep("vector", lambda E: E.tensor_copy(s8b[:], bank(1, p1=8, cols=128)))
    sc.dep("vector", lambda E: E.reduce_max(r8a[:], s8a[:], axis=AXX))
    v = sc.dep("vector", lambda E: E.reduce_max(r8b[:], s8b[:], axis=AXX))
    v = sc.dep("vector", lambda E: E.tensor_tensor(r8a[:], r8a[:], r8b[:], ALU.mult))
    W("scalar", "v", v)
    s = sc.do("scalar", lambda E: E.activation(r8b[:], r8a[:], AF.Sqrt), "s")
    W("tensor", "s", s)
    t = sc.do("tensor", lambda E: E.transpose(bank(2, p1=1, cols=8), r8b[0:8, :], ident_f[0:8, 0:8]), "t")
    W("vector", "t", t)
    sc.dep("vector", lambda E: E.tensor_copy(row8[:], bank(2, p1=1, cols=8)))
    v = sc.dep("vector", lambda E: E.tensor_scalar(row8b[:], row8[:], -1.5, None, ALU.mult))
    v = sc.dep("vector", lambda E: E.tensor_copy(row8c[:], row8b[:]))
    W("tensor", "v", v)
    t = sc.do("tensor", lambda E: E.matmul(bank(2, cols=8), ones_b[:, :], row8c[:],
                                           start=True, stop=True), "t")
    W("vector", "t", t)
    v_mh = sc.dep("vector", lambda E: E.tensor_copy(mh[:], bank(2, cols=8)))
    if "mh" in dbg:
        W("gpsimd", "v", v_mh)
        sc.dma("gpsimd", dbg["mh"][:, :], mh[:], "gd")
    if "kTown" in dbg:
        W("gpsimd", "s", s_tr)
        sc.dma("gpsimd", dbg["kTown"][:, :], kTown[:].rearrange("p a b -> p (a b)"), "gd")
    if "vown" in dbg:
        W("gpsimd", "v", v_mh)
        W("gpsimd", "g", g_ones)
        sc.dma("gpsimd", dbg["vown"][:, :], vga[:].rearrange("p a b -> p (a b)"), "gd")

    W("gpsimd", "v", v_mh)
    W("gpsimd", "s", s_tr)
    W("gpsimd", "g", g_ones)
    ktr = ag_in[0:1, 0:KT_ELS].rearrange("o (hp p c) -> o hp p c", hp=4, p=128)
    g = sc.dma("gpsimd", ktr[0].rearrange("hp p c -> p hp c"), kTown[:], "gd")
    vtr = ag_in[0:1, KT_ELS:].rearrange("o (a p c) -> o a p c", a=NT, p=128)
    g = sc.dma("gpsimd", vtr[0].rearrange("a p c -> p a c"), vga[:], "gd")
    W("gpsimd", "gd", g)
    g_ag = sc.do("gpsimd", lambda E: E.collective_compute(
        "AllGather", ALU.bypass, replica_groups=GROUPS,
        ins=[ag_in.opt()], outs=[ag_out.opt()]), "cc", 1)

    sc.barrier()
    if phases < 4:
        _finish(sc)
        return nc, dbg

    accF = bview(ar1, 0, F32, [65, 16384])
    Ab = [[bview(ar2, int(48.25 * KB) + (2 * hl + pa) * 3 * KB, BF16, [128, 1536])
           for pa in range(2)] for hl in range(2)]
    vr = [bview(ar1, 64 * KB, BF16, [128, NT, H * 65]),
          bview(ar2, 32 * KB, BF16, [128, NT, H * 65])]
    kTr = [bview(ar2, 0, BF16, [128, 4, 2048]),
           bview(ar2, 16 * KB, BF16, [128, 4, 2048])]

    W("sync", "cc", g_ag)
    CHF = [(0, 3), (3, 3), (6, 3), (9, 3), (12, 3), (15, 1)]
    s_bank_last = {0: 0, 3: 0}
    v_accf = 0
    g_kv = {}
    t_r_done = {}
    for r in range(4):
        pr = r % 2
        if r == 0:
            for rr in (0, 1):
                ktro = ag_out[rr:rr + 1, 0:KT_ELS].rearrange("o (hp p c) -> o hp p c", hp=4, p=128)
                sc.dma("sync", kTr[rr % 2][:], ktro[0].rearrange("hp p c -> p hp c"), f"ldk{rr % 2}")
                vtro = ag_out[rr:rr + 1, KT_ELS:].rearrange("o (a p c) -> o a p c", a=NT, p=128)
                g_kv[rr] = sc.dma("sync", vr[rr % 2][:], vtro[0].rearrange("a p c -> p a c"), f"ldk{rr % 2}")
        elif r < 3:
            W("sync", "t", t_r_done[r - 1])
            rr = r + 1
            ktro = ag_out[rr:rr + 1, 0:KT_ELS].rearrange("o (hp p c) -> o hp p c", hp=4, p=128)
            sc.dma("sync", kTr[rr % 2][:], ktro[0].rearrange("hp p c -> p hp c"), f"ldk{rr % 2}")
            vtro = ag_out[rr:rr + 1, KT_ELS:].rearrange("o (a p c) -> o a p c", a=NT, p=128)
            g_kv[rr] = sc.dma("sync", vr[rr % 2][:], vtro[0].rearrange("a p c -> p a c"), f"ldk{rr % 2}")
        W("tensor", f"ldk{r % 2}", g_kv[r])
        if r == 0:
            W("scalar", "v", v_mh)
        for hp in range(4):
            for ib in range(4):
                s_exp_at = {}
                t_av_buf = {(0, 0): 0, (0, 1): 0, (1, 0): 0, (1, 1): 0}
                t_mm = {}
                bset = {}
                for ci, (j0, nj) in enumerate(CHF):
                    for hl in range(2):
                        b0 = 3 * hl
                        bset[(hl, ci)] = b0
                        W("tensor", "s", s_bank_last.get(b0, 0))
                        for jj in range(nj):
                            t = sc.do("tensor", lambda E, hl=hl, hp=hp, j0=j0, jj=jj, ib=ib, b0=b0, pr=pr: E.matmul(
                                bank(b0 + jj), kTr[pr][hl * 64:(hl + 1) * 64, hp, (j0 + jj) * 128:(j0 + jj + 1) * 128],
                                qTp[hl * 64:(hl + 1) * 64, hp, ib * 512:(ib + 1) * 512],
                                start=True, stop=True,
                                tile_position=(hl * 64, 0)), "t")
                        t_mm[hl] = t
                    for hl in range(2):
                        h = hp * 2 + hl
                        b0 = bset[(hl, ci)]
                        abuf = Ab[hl][ci % 2]
                        W("scalar", "t", t_mm[hl])
                        W("scalar", "t", t_av_buf[(hl, ci % 2)])
                        s_e = sc.do("scalar", lambda E, abuf=abuf, b0=b0, nj=nj, h=h: E.activation(
                            abuf[:, 0: nj * 512], bank(b0, cols=nj * 512), AF.Exp,
                            bias=mh[:, h:h + 1]), "s")
                        s_bank_last[b0] = s_e
                        s_exp_at[(hl, ci)] = s_e
                    for hl in range(2):
                        if ci == 0:
                            continue
                        _emit_av(sc, W, bank, vr, pr, qTp, accF, CHF, ci - 1, hl, hp,
                                 s_exp_at, t_av_buf, Ab, v_accf)
                for hl in range(2):
                    _emit_av(sc, W, bank, vr, pr, qTp, accF, CHF, 5, hl, hp,
                             s_exp_at, t_av_buf, Ab, v_accf)
                for hl in range(2):
                    h = hp * 2 + hl
                    W("vector", "t", t_av_buf[(hl, 5 % 2)])
                    col = (h * 4 + ib) * 512
                    if r == 0:
                        v_accf = sc.do("vector", lambda E, hl=hl, col=col: E.tensor_copy(
                            accF[:, col: col + 512], bank(6 + hl, p1=65)), "v")
                    else:
                        v_accf = sc.do("vector", lambda E, hl=hl, col=col: E.tensor_tensor(
                            accF[:, col: col + 512], bank(6 + hl, p1=65),
                            accF[:, col: col + 512], ALU.add), "v")
        t_r_done[r] = sc.cnt["t"]
    if "accF" in dbg:
        W("gpsimd", "v", v_accf)
        sc.dma("gpsimd", dbg["accF"][:, :], accF[:], "gd")

    sc.barrier()

    oaf = bview(ar2, 0, BF16, [128, NT, H * 64])
    zbuf = bview(ar2, 32 * KB, BF16, [128, 2560])
    zT = bview(ar2, 40 * KB, BF16, [128, 20, 128])
    WoBb = bview(ar1, 20 * KB, BF16, [128, 20, 1024])

    t_ep = 0
    v_ep = 0
    for h in range(H):
        for tt in range(NT):
            ib, sub = tt // 4, tt % 4
            col = (h * 4 + ib) * 512 + sub * 128
            W("tensor", "v", v_ep)
            t_ep = sc.do("tensor", lambda E, col=col: E.transpose(
                bank(0, p1=128, cols=65), accF[0:65, col:col + 128], ident_f[0:65, 0:65]), "t")
            W("vector", "t", t_ep)
            sc.do("vector", lambda E, h=h, tt=tt: E.tensor_copy(
                oaf[:, tt, h * 64:(h + 1) * 64], bank(0, cols=64)), "v")
            v_ep = sc.do("vector", lambda E, h=h, tt=tt: E.tensor_copy(
                ztok[:, tt, h:h + 1], bank(0, cols=1, off=64)), "v")
    v_rz2 = sc.dep("vector", lambda E: E.reciprocal(
        rztok[:].rearrange("p a b -> p (a b)"), ztok[:].rearrange("p a b -> p (a b)")))
    for tt in range(NT):
        v_mz = sc.dep("vector", lambda E, tt=tt: E.tensor_tensor(
            mdz[:, tt, :].rearrange("p (h e) -> p h e", h=8),
            mdA[:, tt, :].rearrange("p (h e) -> p h e", h=8),
            rztok[:, tt, :].rearrange("p (h e) -> p h e", e=1).broadcast_to([128, 8, 5]),
            ALU.mult))
    if "oaf" in dbg:
        W("gpsimd", "v", v_ep)
        sc.dma("gpsimd", dbg["oaf"][:, :], oaf[:].rearrange("p a b -> p (a b)"), "gd")
    if "rzt" in dbg:
        W("gpsimd", "v", v_rz2)
        sc.dma("gpsimd", dbg["rzt"][:, :], rztok[:].rearrange("p a b -> p (a b)"), "gd")

    W("sync", "v", max(v_ep, v_rz2))
    for kc in range(20):
        staged(WoB[kc * 128:(kc + 1) * 128, :], WoBb[:, kc, :], kc, stgB[kc % 2][:, 0:1024])
    v_wo = max(cast_m[0], cast_m[1])

    yf2 = bview(ar1, 60 * KB, F32, [128, 1024])
    res2 = bview(ar1, 64 * KB, F32, [128, 1024])
    sq2 = bview(ar1, 68 * KB, F32, [128, 1024])
    out_t = bview(ar1, 72 * KB, F32, [128, 1024])

    outr = out.rearrange("(a p) d -> a p d", p=128)
    zT2 = [zT, bview(ar2, 45 * KB, BF16, [128, 20, 128])]
    zbuf2 = [zbuf, bview(ar2, 50 * KB, BF16, [128, 2560])]
    s_o = 0
    g_out = 0
    v_res = 0
    v_zt = {0: 0, 1: 0}
    t_wo = {0: 0, 1: 0}
    for tt in range(NT):
        par = tt % 2
        zb, zt_ = zbuf2[par], zT2[par]
        W("vector", "v", v_mz)
        W("vector", "t", t_wo[par])
        v_z = sc.dep("vector", lambda E, tt=tt, zb=zb: E.tensor_tensor(
            zb[:].rearrange("p (h e d) -> p h e d", h=8, e=5),
            oaf[:, tt, :].rearrange("p (h o d) -> p h o d", h=8, o=1).broadcast_to([128, 8, 5, 64]),
            mdz[:, tt, :].rearrange("p (h e o) -> p h e o", h=8, o=1).broadcast_to([128, 8, 5, 64]),
            ALU.mult))
        for kc in range(20):
            tb = 6 + (kc % 2)
            W("tensor", "v", v_z if kc < 2 else v_zt[kc % 2])
            t = sc.do("tensor", lambda E, kc=kc, tb=tb, zb=zb: E.transpose(
                bankb(tb, cols=128), zb[:, kc * 128:(kc + 1) * 128], ident_b), "t")
            W("vector", "t", t)
            v_zt[kc % 2] = sc.do("vector", lambda E, kc=kc, tb=tb, zt_=zt_: E.tensor_copy(
                zt_[:, kc, :], bankb(tb, cols=128)), "v")
        v_zt_done = max(v_zt.values())
        W("tensor", "v", v_zt_done)
        W("tensor", "g", v_wo)
        W("tensor", "v", v_res)
        for mc in range(2):
            for kc in range(20):
                t = sc.do("tensor", lambda E, mc=mc, kc=kc, par=par, zt_=zt_: E.matmul(
                    bank(2 * par + mc, cols=512), zt_[:, kc, :], WoBb[:, kc, mc * 512:(mc + 1) * 512],
                    start=(kc == 0), stop=(kc == 19)), "t")
        t_wo[par] = t
        W("sync", "v", v_res)
        vx = sc.dma("sync", yf2[:], ydr[tt], "ldm")
        W("vector", "ldm", vx)
        W("vector", "t", t)
        W("vector", "s", s_o)
        for mc in range(2):
            v_res = sc.do("vector", lambda E, mc=mc, par=par: E.tensor_tensor(
                res2[:, mc * 512:(mc + 1) * 512], bank(2 * par + mc, cols=512),
                yf2[:, mc * 512:(mc + 1) * 512], ALU.add), "v")
        sc.dep("vector", lambda E: E.reduce_sum(lnA[:], res2[:], axis=AXX))
        W("scalar", "v", sc.cnt["v"])
        W("scalar", "s", s_o)
        s_sq = sc.do("scalar", lambda E: E.activation(sq2[:], res2[:], AF.Square,
                                                      accum_out=lnB[:]), "s")
        W("vector", "s", s_sq)
        sc.dep("vector", lambda E: E.tensor_scalar(lnC[:], lnA[:], 1.0 / D, None, ALU.mult))
        sc.dep("vector", lambda E: E.tensor_scalar(lnD[:], lnB[:], 1.0 / D, None, ALU.mult))
        sc.dep("vector", lambda E: E.tensor_tensor(lnG[:], lnC[:], lnC[:], ALU.mult))
        v = sc.dep("vector", lambda E: E.tensor_tensor(lnD[:], lnD[:], lnG[:], ALU.subtract))
        W("scalar", "v", v)
        W("scalar", "s", s_sq)
        ss = sc.do("scalar", lambda E: E.activation(lnD[:], lnD[:], AF.Sqrt, bias=epsc[:, 0:1]), "s")
        W("vector", "s", ss)
        sc.dep("vector", lambda E: E.reciprocal(lnE_[:], lnD[:]))
        sc.dep("vector", lambda E: E.tensor_tensor(lnF[:], lnC[:], lnE_[:], ALU.mult))
        v_ln = sc.dep("vector", lambda E: E.tensor_scalar(lnF[:], lnF[:], -1.0, None, ALU.mult))
        W("scalar", "v", v_ln)
        W("scalar", "gd", g_out)
        s_o = sc.do("scalar", lambda E: E.activation(out_t[:], res2[:], AF.Identity,
                                                     bias=lnF[:, 0:1], scale=lnE_[:, 0:1]), "s")
        W("gpsimd", "s", s_o)
        g_out = sc.dma("gpsimd", outr[tt], out_t[:], "gd")

    _finish(sc)
    return nc, dbg


def _finish(sc):
    sc.wait("gpsimd", "gd", sc.cnt.get("gd", 0))
    sc.replay()


def kernel(**inputs):
    x = inputs["L0_attn_x"]
    Wq, Wk, Wv = inputs["L0_attn_Wq"], inputs["L0_attn_Wk"], inputs["L0_attn_Wv"]
    Wqf, Wkvf = inputs["L0_ffn_Wq"], inputs["L0_ffn_Wkv"]
    Ws, Wd, Wo = inputs["L0_ffn_Ws"], inputs["L0_ffn_Wd"], inputs["L0_ffn_Wo"]

    WaT = np.ascontiguousarray(np.concatenate([Wq.T, Wk.T, Wv.T], axis=1)).astype(np.float32)
    Wsd = np.ascontiguousarray(np.concatenate([Ws, Wd], axis=1)).astype(np.float32)
    WoB = np.ascontiguousarray(
        np.tile(np.asarray(Wo).reshape(DH, E, D).transpose(1, 0, 2).reshape(E * DH, D), (H, 1)))

    nc, dbg = build()
    in_maps = []
    for c in range(8):
        b, q = c // 4, c % 4
        xb = np.asarray(x[b], dtype=np.float32)
        xTb = np.ascontiguousarray(xb.T)
        in_maps.append({
            "xT": xTb,
            "xqT": np.ascontiguousarray(xTb[:, q * SQ:(q + 1) * SQ]),
            "xq": np.ascontiguousarray(xb[q * SQ:(q + 1) * SQ]),
            "WaT": WaT, "Wsd": Wsd,
            "Wqf": np.ascontiguousarray(np.asarray(Wqf, dtype=np.float32)),
            "Wkvf": np.ascontiguousarray(np.asarray(Wkvf, dtype=np.float32)),
            "WoB": WoB.astype(np.float32),
        })
    res = run_bass_kernel_spmd(nc, in_maps, core_ids=list(range(8)))
    outp = np.empty((2, S, D), dtype=np.float32)
    for c in range(8):
        b, q = c // 4, c % 4
        outp[b, q * SQ:(q + 1) * SQ] = res.results[c]["out"]
    return outp


# version 13
# speedup vs baseline: 1.0896x; 1.0098x over previous
import numpy as np
import ml_dtypes
import concourse.bass as bass
import concourse.mybir as mybir
from concourse.bass_utils import run_bass_kernel_spmd

F32 = mybir.dt.float32
BF16 = mybir.dt.bfloat16
AF = mybir.ActivationFunctionType
ALU = mybir.AluOpType
AXX = mybir.AxisListType.X

S = 8192
SQ = 2048
D = 1024
DH = 64
H = 8
E = 5
NT = 16
GROUPS = [[0, 1, 2, 3], [4, 5, 6, 7]]
ENGS = ("sync", "gpsimd", "vector", "scalar", "tensor")
KT_ELS = 512 * SQ
V_ELS = SQ * (H * 65)
BLOB_ELS = KT_ELS + V_ELS
KB = 1024

BUILD_PHASES = 4


class Sched:
    def __init__(self, nc):
        self.nc = nc
        self.q = {e: [] for e in ENGS}
        self.sems = {}
        self.cnt = {}

    def sem(self, name):
        if name not in self.sems:
            self.sems[name] = self.nc.semaphore(name).__enter__()
            self.cnt[name] = 0
        return self.sems[name]

    def on(self, eng, fn):
        self.q[eng].append(fn)

    def wait(self, eng, sem, val):
        if not val:
            return
        h = self.sem(sem)
        self.on(eng, lambda E, h=h, v=val: E.wait_ge(h, v))

    def do(self, eng, fn, sem=None, inc=1):
        if sem is not None:
            h = self.sem(sem)
            self.cnt[sem] += inc
            v = self.cnt[sem]
            self.on(eng, lambda E, fn=fn, h=h, inc=inc: fn(E).then_inc(h, inc))
            return v
        self.on(eng, lambda E, fn=fn: fn(E))
        return None

    ENGSEM = {"vector": "v", "scalar": "s", "tensor": "t", "gpsimd": "g", "sync": "y"}

    def dep(self, eng, fn):
        m = self.ENGSEM[eng]
        self.wait(eng, m, self.cnt.get(m, 0))
        return self.do(eng, fn, m)

    def dma(self, eng, out, in_, sem="gd"):
        return self.do(eng, lambda E, o=out, i=in_: E.dma_start(out=o, in_=i), sem, 16)

    def barrier(self):
        self.wait("gpsimd", "gd", self.cnt.get("gd", 0))
        h = self.sem("bar")
        self.cnt["bar"] += len(ENGS)
        v = self.cnt["bar"]
        for e in ENGS:
            self.on(e, lambda E, h=h: E.nop().then_inc(h, 1))
            self.on(e, lambda E, h=h, v=v: E.wait_ge(h, v))

    def replay(self):
        with self.nc.Block() as block:
            @block.sync
            def _(E):
                for fn in self.q["sync"]:
                    fn(E)

            @block.gpsimd
            def _(E):
                for fn in self.q["gpsimd"]:
                    fn(E)

            @block.vector
            def _(E):
                for fn in self.q["vector"]:
                    fn(E)

            @block.scalar
            def _(E):
                for fn in self.q["scalar"]:
                    fn(E)

            @block.tensor
            def _(E):
                for fn in self.q["tensor"]:
                    fn(E)


def bview(arena, byte_off, dtype, shape):
    esz = mybir.dt.size(dtype)
    assert byte_off % esz == 0
    ap = arena[:].bitcast(dtype)
    parts, cols = shape[0], int(np.prod(shape[1:]))
    v = ap[0:parts, byte_off // esz: byte_off // esz + cols]
    if len(shape) > 2:
        names = " ".join(f"d{i}" for i in range(1, len(shape)))
        v = v.rearrange(f"p ({names}) -> p {names}",
                        **{f"d{i}": shape[i] for i in range(1, len(shape))})
    return v



def _emit_av(sc, W, bank, vr, pr, qTp, accF, CHF, ci, hl, hp, s_exp_at, t_av_buf, Ab, v_accf):
    j0, nj = CHF[ci]
    bav = 6 + hl
    h = hp * 2 + hl
    abuf = Ab[hl][ci % 2]
    W("tensor", "s", s_exp_at[(hl, ci)])
    if j0 == 0:
        W("tensor", "v", v_accf)
    ALU = mybir.AluOpType
    t_av = 0
    for jj in range(nj):
        t_av = sc.do("tensor", lambda E, bav=bav, abuf=abuf, jj=jj, j0=j0, h=h, pr=pr: E.matmul(
            bank(bav, p0=0, p1=65), vr[pr][:, j0 + jj, h * 65: h * 65 + 65],
            abuf[:, jj * 512:(jj + 1) * 512],
            start=(j0 == 0 and jj == 0), stop=(j0 + jj == 15)), "t")
    t_av_buf[(hl, ci % 2)] = t_av


def build(debug=(), phases=BUILD_PHASES, acut=99):
    nc = bass.Bass("TRN2", target_bir_lowering=False, debug=False)

    xT = nc.dram_tensor("xT", [D, S], F32, kind="ExternalInput").ap()
    xqT = nc.dram_tensor("xqT", [D, SQ], F32, kind="ExternalInput").ap()
    xq = nc.dram_tensor("xq", [SQ, D], F32, kind="ExternalInput").ap()
    WaT = nc.dram_tensor("WaT", [D, 192], F32, kind="ExternalInput").ap()
    Wsd = nc.dram_tensor("Wsd", [D, 80], F32, kind="ExternalInput").ap()
    Wqf = nc.dram_tensor("Wqf", [D, H * E * DH], F32, kind="ExternalInput").ap()
    Wkvf = nc.dram_tensor("Wkvf", [D, 2 * H * E * DH], F32, kind="ExternalInput").ap()
    WoB = nc.dram_tensor("WoB", [H * E * DH, D], F32, kind="ExternalInput").ap()
    out = nc.dram_tensor("out", [SQ, D], F32, kind="ExternalOutput").ap()

    dbg = {}
    def dbg_t(name, shape, dtype):
        if name in debug:
            dbg[name] = nc.dram_tensor("dbg_" + name, shape, dtype, kind="ExternalOutput").ap()

    dbg_t("acc", [128, 4096], F32)
    dbg_t("oat", [128, NT * DH], F32)
    dbg_t("y", [SQ, D], F32)
    dbg_t("gates", [128, NT * 80], F32)
    dbg_t("mdd", [128, NT * 40], F32)
    dbg_t("kTown", [128, 4 * SQ], BF16)
    dbg_t("vown", [128, NT * H * 65], BF16)
    dbg_t("accF", [65, 16384], F32)
    dbg_t("oaf", [128, NT * H * DH], BF16)
    dbg_t("rzt", [128, NT * H], F32)
    dbg_t("mh", [128, 8], F32)

    ydram = nc.dram_tensor("ydram", [SQ, D], F32).ap()
    rs_in = nc.dram_tensor("rs_in", [256, SQ], F32).ap()
    rs_out = nc.dram_tensor("rs_out", [64, SQ], F32).ap()
    ag_in = nc.dram_tensor("ag_in", [1, BLOB_ELS], BF16).ap()
    ag_out = nc.dram_tensor("ag_out", [4, BLOB_ELS], BF16).ap()

    ar1 = nc.sbuf_tensor("ar1", [128, 100 * KB], mybir.dt.uint8).__enter__()
    ar2 = nc.sbuf_tensor("ar2", [128, 64 * KB], mybir.dt.uint8).__enter__()
    ar3 = nc.sbuf_tensor("ar3", [128, 38 * KB], mybir.dt.uint8).__enter__()
    ps = nc.psum_tensor("ps", [128, 4096], F32).__enter__()
    psB = ps[:].bitcast(BF16)

    def bank(i, p0=0, p1=128, cols=512, off=0):
        return ps[p0:p1, i * 512 + off: i * 512 + off + cols]

    def bankb(i, p0=0, p1=128, cols=512, off=0):
        return psB[p0:p1, i * 1024 + off: i * 1024 + off + cols]

    sc = Sched(nc)
    W = sc.wait
    cast_m = {0: 0, 1: 0}

    def staged(src, dst, e, st_view):
        par = e % 2
        sem = f"ld{par}"
        W("sync", "g", cast_m[par])
        v = sc.dma("sync", st_view, src, sem)
        W("gpsimd", sem, v)
        cast_m[par] = sc.do("gpsimd", lambda E, dst=dst, st=st_view: E.tensor_copy(dst, st), "g")
        return cast_m[par]

    ident_f = bview(ar3, 32 * KB, F32, [128, 128])
    ident_b = bview(ar3, 32 * KB + 512, BF16, [128, 128])
    ones_b = bview(ar3, 32 * KB + 768, BF16, [1, 128])
    onec_b = bview(ar3, 32 * KB + 1024, BF16, [128, 1])
    sm = 34 * KB
    qmaxb = bview(ar3, sm, F32, [128, 1])
    epsc = bview(ar3, sm + 4000, F32, [128, 1])
    kn2 = bview(ar3, sm + 8, F32, [128, 16])
    knr = bview(ar3, sm + 72, F32, [128, 16])
    negmk = bview(ar3, sm + 136, F32, [128, 16])
    zp = bview(ar3, sm + 200, F32, [128, 8])
    zk = bview(ar3, sm + 232, F32, [128, 1])
    rzk = bview(ar3, sm + 236, F32, [128, 1])
    lnA = bview(ar3, sm + 240, F32, [128, 1])
    lnB = bview(ar3, sm + 244, F32, [128, 1])
    lnC = bview(ar3, sm + 248, F32, [128, 1])
    lnD = bview(ar3, sm + 252, F32, [128, 1])
    lnG = bview(ar3, sm + 256, F32, [128, 1])
    lnE_ = bview(ar3, sm + 260, F32, [128, 1])
    lnF = bview(ar3, sm + 264, F32, [128, 1])
    mh = bview(ar3, sm + 272, F32, [128, 8])
    nq = bview(ar3, sm + 304, F32, [128, 8])
    nk = bview(ar3, sm + 336, F32, [128, 8])
    nqt = bview(ar3, sm + 368, F32, [128, 2])
    nkt = bview(ar3, sm + 1000, F32, [128, 2])
    q1 = bview(ar3, sm + 376, F32, [1, 1])
    q2 = bview(ar3, sm + 380, F32, [1, 1])
    q3 = bview(ar3, sm + 384, BF16, [1, 1])
    qrow = bview(ar3, sm + 388, F32, [1, 4])
    s8a = bview(ar3, sm + 512, F32, [8, 128])
    s8b = bview(ar3, sm + 1536, F32, [8, 128])
    r8a = bview(ar3, sm + 2560, F32, [8, 1])
    r8b = bview(ar3, sm + 2564, F32, [8, 1])
    row8 = bview(ar3, sm + 2568, F32, [1, 8])
    row8b = bview(ar3, sm + 2600, F32, [1, 8])
    row8c = bview(ar3, sm + 2632, BF16, [1, 8])
    ztok = bview(ar3, sm + 2648, F32, [128, NT, 8])
    rztok = bview(ar3, sm + 3160, F32, [128, NT, 8])

    ssg = bview(ar3, 0, F32, [128, NT, 40])
    sdg = bview(ar3, 2560 * 1, F32, [128, NT, 40])
    mdA = bview(ar3, 2560 * 2, F32, [128, NT, 40])
    mdz = bview(ar3, 2560 * 3, F32, [128, NT, 40])
    sgraw = bview(ar3, 2560 * 4, F32, [128, 80])
    sgsig = bview(ar3, 2560 * 4 + 320, F32, [128, 80])
    med = bview(ar3, 2560 * 4 + 640, F32, [128, 6, 8])
    gaccq2 = [bview(ar3, 2560 * 4 + 832, F32, [128, 64]), bview(ar3, 12352, F32, [128, 64])]
    gacck2 = [bview(ar3, 2560 * 4 + 1088, F32, [128, 64]), bview(ar3, 12608, F32, [128, 64])]
    gaccv2 = [bview(ar3, 2560 * 4 + 1344, F32, [128, 64]), bview(ar3, 12864, F32, [128, 64])]
    sqscr = bview(ar3, 2560 * 4 + 1600, F32, [128, 64])
    sqscr2 = bview(ar3, 2560 * 4 + 1856, F32, [128, 64])

    v = sc.do("gpsimd", lambda E: E.memset(ident_f, 0.0), "g")
    v = sc.do("gpsimd", lambda E: E.memset(ident_b, 0.0), "g")
    v = sc.do("gpsimd", lambda E: E.memset(ones_b, 1.0), "g")
    v = sc.do("gpsimd", lambda E: E.memset(onec_b, 1.0), "g")
    v = sc.do("gpsimd", lambda E: E.memset(epsc, 1e-5), "g")
    W("gpsimd", "g", v)
    v = sc.do("gpsimd", lambda E: E.affine_select(
        out=ident_f, in_=ident_f, compare_op=ALU.not_equal,
        fill=1.0, base=0, pattern=[[-1, 128]], channel_multiplier=1), "g")
    g_setup = sc.do("gpsimd", lambda E: E.affine_select(
        out=ident_b, in_=ident_b, compare_op=ALU.not_equal,
        fill=1.0, base=0, pattern=[[-1, 128]], channel_multiplier=1), "g")
    W("gpsimd", "g", g_setup)

    stg = [bview(ar1, 0, F32, [128, 2048]), bview(ar1, 8 * KB, F32, [128, 2048])]
    xbbf = bview(ar1, 16 * KB, BF16, [128, 8, 2048])
    QT = bview(ar1, 48 * KB, BF16, [128, 8192])
    acc = bview(ar1, 64 * KB, F32, [128, 4096])
    Apr = bview(ar1, 80 * KB, BF16, [128, 8192])
    KT = bview(ar2, 0, BF16, [128, 2048])
    Ktok = bview(ar2, 4 * KB, BF16, [128, 16, 64])
    Vtok = bview(ar2, 6 * KB, BF16, [128, 16, 64])
    Vhat = bview(ar2, 8 * KB, BF16, [128, 64])
    WaTb = bview(ar2, 9 * KB, BF16, [128, 8, 192])
    sqb = bview(ar2, 13 * KB, BF16, [64, 2048])
    oat = bview(ar2, 18 * KB, F32, [128, NT, 64])
    arstage = bview(ar2, 23 * KB, F32, [64, 2048])

    if acut <= -1:
        _finish(sc)
        return nc, dbg
    for e in range(8):
        staged(WaT[e * 128:(e + 1) * 128, :], WaTb[:, e, :], e, stg[e % 2][:, 0:192])

    if acut <= 0:
        _finish(sc)
        return nc, dbg
    for e in range(8):
        staged(xqT[e * 128:(e + 1) * 128, :], xbbf[:, e, :], e, stg[e % 2])
    v_cast = cast_m[1]
    if acut <= 0.3:
        _finish(sc)
        return nc, dbg
    W("tensor", "g", v_cast)
    W("tensor", "g", g_setup)
    ev_bank = {}
    tr_ev = 0
    s_kn = 0
    for blk in range(16):
        pj = blk % 4
        W("tensor", "v", ev_bank.get(pj, 0))
        W("tensor", "s", s_kn if blk >= 4 else 0)
        for e in range(8):
            tmm = sc.do("tensor", lambda E, e=e, blk=blk, pj=pj: E.matmul(
                bank(pj, cols=128), xbbf[:, e, blk * 128:(blk + 1) * 128],
                WaTb[:, e, 64:192], start=(e == 0), stop=(e == 7)), "t")
        W("vector", "t", tmm)
        sc.do("vector", lambda E, pj=pj, blk=blk: E.tensor_copy(
            Ktok[:, blk, :], bank(pj, cols=64)), "v")
        ev = sc.do("vector", lambda E, pj=pj, blk=blk: E.tensor_copy(
            Vtok[:, blk, :], bank(pj, cols=64, off=64)), "v")
        ev_bank[pj] = ev
        if acut > 0.8:
            W("tensor", "v", ev)
            t = sc.do("tensor", lambda E, blk=blk: E.transpose(
                bankb(5, p1=64, cols=128), Ktok[:, blk, :], ident_b), "t")
            W("vector", "t", t)
            tr_ev = sc.do("vector", lambda E, blk=blk: E.tensor_copy(
                KT[0:64, blk * 128: blk * 128 + 128], bankb(5, p1=64, cols=128)), "v")
            W("tensor", "v", tr_ev)
    s_kn_done = s_kn
    if acut <= 1:
        _finish(sc)
        return nc, dbg
    W("tensor", "s", s_kn_done)

    t_qband = sc.cnt["t"]
    v_qev = 0
    s_sq = 0
    t_qn = 0
    v_qn = 0
    qrow_init = True
    for band in range(4):
        for e in range(8):
            if e == 0:
                W("gpsimd", "t", t_qband)
            staged(xT[e * 128:(e + 1) * 128, band * 2048:(band + 1) * 2048],
                   xbbf[:, e, :], e, stg[e % 2])
        W("tensor", "g", cast_m[e % 2])
        W("tensor", "v", v_qev)
        W("tensor", "s", s_sq)
        for sub in range(4):
            for e in range(8):
                t_qband = sc.do("tensor", lambda E, sub=sub, e=e: E.matmul(
                    bank(sub, p1=64), WaTb[:, e, 0:64], xbbf[:, e, sub * 512:(sub + 1) * 512],
                    start=(e == 0), stop=(e == 7)), "t")
        W("vector", "t", t_qband)
        for sub in range(4):
            v_qev = sc.do("vector", lambda E, band=band, sub=sub: E.tensor_copy(
                QT[0:64, band * 2048 + sub * 512: band * 2048 + (sub + 1) * 512],
                bank(sub, p1=64)), "v")
        s_sq = s_sq
    if acut <= 2:
        _finish(sc)
        return nc, dbg
    if acut <= 3:
        _finish(sc)
        return nc, dbg
    CH = [(0, 1536), (1536, 1536), (3072, 1536), (4608, 1536), (6144, 1536), (7680, 512)]
    s_exp_set = {0: 0, 3: 0}
    v_rmx_set = {0: 0, 3: 0}
    W("tensor", "v", sc.cnt["v"])
    av_evac = sc.cnt["v"]
    prev_avv = {6: 0, 7: 0}
    rmxp = zp
    for kblk in range(16):
        for ci, (ibase, fd) in enumerate(CH):
            b0 = 0 if ci % 2 == 0 else 3
            W("tensor", "s", s_exp_set[b0])
            W("tensor", "v", v_rmx_set[b0])
            for sub in range(fd // 512):
                t = sc.do("tensor", lambda E, kblk=kblk, b0=b0, sub=sub, ibase=ibase: E.matmul(
                    bank(b0 + sub), KT[0:64, kblk * 128:(kblk + 1) * 128],
                    QT[0:64, ibase + sub * 512: ibase + (sub + 1) * 512],
                    start=True, stop=True), "t")
            W("vector", "t", t)
            v_rmx_set[b0] = sc.dep("vector", lambda E, b0=b0, fd=fd, ci=ci: E.reduce_max(
                knr[:, ci: ci + 1], bank(b0, cols=fd), axis=AXX))
        v = sc.dep("vector", lambda E: E.reduce_max(rzk[:], knr[:, 0:6], axis=AXX))
        v_negm = sc.dep("vector", lambda E, kblk=kblk: E.tensor_scalar(
            negmk[:, kblk: kblk + 1], rzk[:], -1.0, None, ALU.mult))
        W("scalar", "v", v_negm)
        for ci, (ibase, fd) in enumerate(CH):
            b0 = 0 if ci % 2 == 0 else 3
            cidx = kblk * 6 + ci
            if ci >= 2:
                W("tensor", "s", s_exp_set[b0])
            else:
                W("tensor", "v", av_evac)
            W("tensor", "v", v_rmx_set[b0])
            for sub in range(fd // 512):
                t = sc.do("tensor", lambda E, kblk=kblk, b0=b0, sub=sub, ibase=ibase: E.matmul(
                    bank(b0 + sub), KT[0:64, kblk * 128:(kblk + 1) * 128],
                    QT[0:64, ibase + sub * 512: ibase + (sub + 1) * 512],
                    start=True, stop=True), "t")
            W("scalar", "t", t)
            s_exp_set[b0] = sc.do("scalar", lambda E, b0=b0, fd=fd, ibase=ibase, ci=ci, kblk=kblk: E.activation(
                Apr[:, ibase: ibase + fd], bank(b0, cols=fd), AF.Exp,
                bias=negmk[:, kblk: kblk + 1], accum_out=zp[:, ci: ci + 1]), "s")
        W("vector", "s", s_exp_set[b0])
        sc.dep("vector", lambda E: E.reduce_sum(zk[:], zp[:, 0:6], axis=AXX))
        v_rz = sc.dep("vector", lambda E: E.reciprocal(rzk[:], zk[:]))
        W("scalar", "v", v_rz)
        s_vh = sc.do("scalar", lambda E, kblk=kblk: E.mul(Vhat[:], Vtok[:, kblk, :], rzk[:, 0:1]), "s")
        W("tensor", "s", s_vh)
        for pr in range(8):
            bav = 6 + (pr % 2)
            W("tensor", "v", prev_avv[bav])
            t1 = sc.do("tensor", lambda E, bav=bav, pr=pr: E.matmul(
                bank(bav, p0=0, p1=64), Vhat[:], Apr[:, (2 * pr) * 512:(2 * pr + 1) * 512],
                start=True, stop=True, tile_position=(0, 0)), "t")
            t2 = sc.do("tensor", lambda E, bav=bav, pr=pr: E.matmul(
                bank(bav, p0=64, p1=128), Vhat[:], Apr[:, (2 * pr + 1) * 512:(2 * pr + 2) * 512],
                start=True, stop=True, tile_position=(0, 64)), "t")
            W("vector", "t", t2)
            if kblk == 0:
                ev = sc.do("vector", lambda E, bav=bav, pr=pr: E.tensor_copy(
                    acc[:, pr * 512:(pr + 1) * 512], bank(bav)), "v")
            else:
                ev = sc.do("vector", lambda E, bav=bav, pr=pr: E.tensor_tensor(
                    acc[:, pr * 512:(pr + 1) * 512], bank(bav),
                    acc[:, pr * 512:(pr + 1) * 512], ALU.add), "v")
            prev_avv[bav] = ev
        av_evac = sc.cnt["v"]

    if acut <= 4:
        _finish(sc)
        return nc, dbg
    if "acc" in dbg:
        sc.wait("gpsimd", "v", av_evac)
        sc.dma("gpsimd", dbg["acc"][:, :], acc[:], "gd")

    W("gpsimd", "v", av_evac)
    for qr in range(4):
        for prl in range(2):
            for par in range(2):
                g = sc.dma("gpsimd",
                           rs_in[qr * 64:(qr + 1) * 64, prl * 1024 + par * 512: prl * 1024 + (par + 1) * 512],
                           acc[par * 64:(par + 1) * 64, (2 * qr + prl) * 512:(2 * qr + prl + 1) * 512],
                           "gd")
    W("gpsimd", "gd", g)
    g_cc = sc.do("gpsimd", lambda E: E.collective_compute(
        "ReduceScatter", ALU.add, replica_groups=GROUPS,
        ins=[rs_in.opt()], outs=[rs_out.opt()]), "cc", 1)

    if acut <= 5:
        _finish(sc)
        return nc, dbg
    W("sync", "cc", g_cc)
    v = sc.dma("sync", arstage[:, :], rs_out[:, :], "ldm")
    W("tensor", "ldm", v)
    for tt in range(NT):
        t = sc.do("tensor", lambda E, tt=tt: E.transpose(
            bank(0, cols=64, off=(tt % 4) * 64), arstage[0:64, tt * 128:(tt + 1) * 128],
            ident_f[0:64, 0:64]), "t")
        if tt % 4 == 3:
            W("vector", "t", t)
            for u in range(4):
                v = sc.do("vector", lambda E, tt=tt, u=u: E.tensor_copy(
                    oat[:, tt - 3 + u, :], bank(0, cols=64, off=u * 64)), "v")
            W("tensor", "v", v)
    if "oat" in dbg:
        W("gpsimd", "v", sc.cnt["v"])
        sc.dma("gpsimd", dbg["oat"][:, :], oat[:].rearrange("p a b -> p (a b)"), "gd")

    sc.barrier()
    if phases < 2:
        _finish(sc)
        return nc, dbg

    xq_t = bview(ar1, 0, F32, [128, 1024])
    z1 = bview(ar1, 4 * KB, F32, [128, 1024])
    sqz = bview(ar1, 8 * KB, F32, [128, 1024])
    ybf_t = bview(ar1, 12 * KB, BF16, [128, 1024])
    yf_t = bview(ar1, 14 * KB, F32, [128, 1024])
    yT = bview(ar2, 32 * KB, BF16, [128, 8, 2048])

    xqr = xq.rearrange("(a p) d -> a p d", p=128)
    ydr = ydram.rearrange("(a p) d -> a p d", p=128)
    ybf2 = [ybf_t, bview(ar1, 18 * KB, BF16, [128, 1024])]
    s_y = 0
    g_y = 0
    v_ytr = {0: 0, 1: 0, 2: 0, 3: 0}
    s_y_at = {}
    pend_y = None
    for tt in range(NT):
        par = tt % 2
        ybf = ybf2[par]
        W("sync", "s", s_y)
        vx = sc.dma("sync", xq_t[:], xqr[tt], "ldm")
        W("vector", "ldm", vx)
        W("vector", "s", s_y)
        v = sc.do("vector", lambda E, tt=tt: E.tensor_tensor(
            z1[:].rearrange("p (r d) -> p r d", r=16),
            xq_t[:].rearrange("p (r d) -> p r d", r=16),
            oat[:, tt, :].rearrange("p (r d) -> p r d", r=1).broadcast_to([128, 16, 64]),
            ALU.add), "v")
        sc.dep("vector", lambda E: E.reduce_sum(lnA[:], z1[:], axis=AXX))
        W("scalar", "v", sc.cnt["v"])
        W("scalar", "s", s_y)
        s_sq = sc.do("scalar", lambda E: E.activation(sqz[:], z1[:], AF.Square,
                                                      accum_out=lnB[:]), "s")
        W("vector", "s", s_sq)
        sc.dep("vector", lambda E: E.tensor_scalar(lnC[:], lnA[:], 1.0 / D, None, ALU.mult))
        sc.dep("vector", lambda E: E.tensor_scalar(lnD[:], lnB[:], 1.0 / D, None, ALU.mult))
        sc.dep("vector", lambda E: E.tensor_tensor(lnG[:], lnC[:], lnC[:], ALU.mult))
        v = sc.dep("vector", lambda E: E.tensor_tensor(lnD[:], lnD[:], lnG[:], ALU.subtract))
        W("scalar", "v", v)
        W("scalar", "s", s_sq)
        ss = sc.do("scalar", lambda E: E.activation(lnD[:], lnD[:], AF.Sqrt, bias=epsc[:, 0:1]), "s")
        W("vector", "s", ss)
        sc.dep("vector", lambda E: E.reciprocal(lnE_[:], lnD[:]))
        sc.dep("vector", lambda E: E.tensor_tensor(lnF[:], lnC[:], lnE_[:], ALU.mult))
        v_ln = sc.dep("vector", lambda E: E.tensor_scalar(lnF[:], lnF[:], -1.0, None, ALU.mult))
        W("scalar", "v", v_ln)
        W("scalar", "gd", g_y)
        W("scalar", "v", v_ytr[par])
        sc.do("scalar", lambda E, ybf=ybf: E.activation(ybf[:], z1[:], AF.Identity,
                                                        bias=lnF[:, 0:1], scale=lnE_[:, 0:1]), "s")
        s_y = sc.do("scalar", lambda E: E.activation(yf_t[:], z1[:], AF.Identity,
                                                     bias=lnF[:, 0:1], scale=lnE_[:, 0:1]), "s")
        s_y_at[tt] = s_y
        W("gpsimd", "s", s_y)
        g_y = sc.dma("gpsimd", ydr[tt], yf_t[:], "gd")
        if "y" in dbg:
            g_y = sc.dma("gpsimd", dbg["y"].rearrange("(a p) d -> a p d", p=128)[tt], yf_t[:], "gd")
        if pend_y is not None:
            ptt = pend_y
            W("tensor", "s", s_y_at[ptt])
            for e in range(8):
                tb = 4 + (e % 4)
                W("tensor", "v", v_ytr[e % 4])
                t = sc.do("tensor", lambda E, e=e, tb=tb, pybf=ybf2[ptt % 2]: E.transpose(
                    bankb(tb, cols=128), pybf[:, e * 128:(e + 1) * 128], ident_b), "t")
                W("vector", "t", t)
                v_ytr[e % 4] = sc.do("vector", lambda E, e=e, tt=ptt, tb=tb: E.tensor_copy(
                    yT[:, e, tt * 128:(tt + 1) * 128], bankb(tb, cols=128)), "v")
        pend_y = tt
    ptt = pend_y
    W("tensor", "s", s_y_at[ptt])
    for e in range(8):
        tb = 4 + (e % 4)
        W("tensor", "v", v_ytr[e % 4])
        t = sc.do("tensor", lambda E, e=e, tb=tb, pybf=ybf2[ptt % 2]: E.transpose(
            bankb(tb, cols=128), pybf[:, e * 128:(e + 1) * 128], ident_b), "t")
        W("vector", "t", t)
        v_ytr[e % 4] = sc.do("vector", lambda E, e=e, tt=ptt, tb=tb: E.tensor_copy(
            yT[:, e, tt * 128:(tt + 1) * 128], bankb(tb, cols=128)), "v")

    sc.barrier()
    if phases < 3:
        _finish(sc)
        return nc, dbg

    stgB = [bview(ar1, 0, F32, [128, 1024]), bview(ar1, 4 * KB, F32, [128, 1024])]
    Wsdb = bview(ar3, 13824 + 4096, BF16, [128, 8, 80])
    gtmp = bview(ar3, 19200, F32, [128, 5, 64])
    for e in range(8):
        staged(Wsd[e * 128:(e + 1) * 128, :], Wsdb[:, e, :], e, stgB[e % 2][:, 0:80])
    W("tensor", "g", cast_m[e % 2])
    v_mask = 0
    s_sg = 0
    for tt in range(NT):
        W("tensor", "v", v_mask)
        W("tensor", "s", s_sg)
        for e in range(8):
            t = sc.do("tensor", lambda E, e=e, tt=tt: E.matmul(
                bank(0, cols=80), yT[:, e, tt * 128:(tt + 1) * 128], Wsdb[:, e, :],
                start=(e == 0), stop=(e == 7)), "t")
        W("vector", "t", t)
        v_sgc = sc.dep("vector", lambda E: E.tensor_copy(sgraw[:], bank(0, cols=80)))
        W("scalar", "t", t)
        W("scalar", "v", v_sgc)
        W("scalar", "s", s_sg)
        s_sg = sc.do("scalar", lambda E: E.activation(sgsig[:], bank(0, cols=80), AF.Sigmoid), "s")
        for half in (0, 1):
            raw = sgraw[:, half * 40: half * 40 + 40].rearrange("p (h e) -> p h e", h=8)
            v0, v1, v2, v3, v4 = (raw[:, :, i:i + 1] for i in range(5))
            m0, m1, m2, m3, m4, m5_ = (med[:, i, :].rearrange("p (h o) -> p h o", o=1)
                                       for i in range(6))
            for (o, a, b, op) in ((m0, v0, v1, ALU.min), (m1, v0, v1, ALU.max),
                                  (m2, v2, v3, ALU.min), (m3, v2, v3, ALU.max),
                                  (m4, m0, m2, ALU.max), (m5_, m1, m3, ALU.min),
                                  (m0, m4, m5_, ALU.min), (m1, m4, m5_, ALU.max),
                                  (m2, m1, v4, ALU.min), (m3, m0, m2, ALU.max)):
                sc.dep("vector", lambda E, o=o, a=a, b=b, op=op: E.tensor_tensor(o, a, b, op))
            medv = med[:, 3, :].rearrange("p (h o) -> p h o", o=1).broadcast_to([128, 8, 5])
            v_mask = sc.dep("vector", lambda E, raw=raw, medv=medv, half=half, tt=tt: E.tensor_tensor(
                (mdA if half else mdz)[:, tt, :].rearrange("p (h e) -> p h e", h=8),
                raw, medv, ALU.is_ge))
        W("vector", "s", s_sg)
        sc.dep("vector", lambda E, tt=tt: E.tensor_tensor(
            ssg[:, tt, :], sgsig[:, 0:40], mdz[:, tt, :], ALU.mult))
        sc.dep("vector", lambda E, tt=tt: E.tensor_tensor(
            sdg[:, tt, :], sgsig[:, 40:80], mdA[:, tt, :], ALU.mult))
        v_mask = sc.dep("vector", lambda E, tt=tt: E.tensor_scalar(
            sdg[:, tt, :], sdg[:, tt, :], 0.125, None, ALU.mult))
    if "gates" in dbg:
        W("gpsimd", "v", v_mask)
        sc.dma("gpsimd", dbg["gates"][:, 0: NT * 40], ssg[:].rearrange("p a b -> p (a b)"), "gd")
        sc.dma("gpsimd", dbg["gates"][:, NT * 40: NT * 80], sdg[:].rearrange("p a b -> p (a b)"), "gd")
    if "mdd" in dbg:
        W("gpsimd", "v", v_mask)
        sc.dma("gpsimd", dbg["mdd"][:, :], mdA[:].rearrange("p a b -> p (a b)"), "gd")

    qgh = bview(ar3, 13824, BF16, [128, 16, 64])
    kgh = bview(ar3, 13824 + 2048, BF16, [128, 16, 64])
    Wqh2 = [bview(ar1, 24 * KB, BF16, [128, 8, 320]), bview(ar1, 39 * KB, BF16, [128, 8, 320])]
    Wkh2 = [bview(ar1, 29 * KB, BF16, [128, 8, 320]), bview(ar1, 44 * KB, BF16, [128, 8, 320])]
    Wvh2 = [bview(ar1, 34 * KB, BF16, [128, 8, 320]), bview(ar1, 49 * KB, BF16, [128, 8, 320])]
    kTown = bview(ar1, 8 * KB, BF16, [128, 4, 2048])
    vga = bview(ar1, 64 * KB, BF16, [128, NT, H * 65])
    qTp = bview(ar1, 82 * KB, BF16, [128, 4, 2048])

    g_ones = sc.do("gpsimd", lambda E: E.memset(
        vga[:].rearrange("p a (h c) -> p a h c", c=65)[:, :, :, 64:65], 1.0), "g", 1)

    v_gate_guard = v_mask
    s_nrm = 0
    v_nrm = 0
    t_tr = 0
    t_h_done = {}
    ev_set = {0: 0, 1: 0}
    gate_done = {0: 0, 1: 0}
    first_nrm = True
    s_tr = 0
    W("tensor", "v", sc.cnt["v"])
    W("tensor", "s", sc.cnt["s"])
    for h in range(H):
        first_nrm = True
        hpar = h % 2
        Wqh, Wkh, Wvh = Wqh2[hpar], Wkh2[hpar], Wvh2[hpar]
        W("gpsimd", "t", t_h_done.get(h - 2, 0))
        for wi, (wsrc, wdst, c0) in enumerate(((Wqf, Wqh, h * 320), (Wkvf, Wkh, h * 320),
                                               (Wkvf, Wvh, 2560 + h * 320))):
            for e in range(8):
                staged(wsrc[e * 128:(e + 1) * 128, c0:c0 + 320], wdst[:, e, :],
                       wi * 8 + e, stgB[(wi * 8 + e) % 2][:, 0:320])
        v_w = max(cast_m[0], cast_m[1])
        W("tensor", "g", v_w)
        hp, hl = h // 2, h % 2
        pend_tr = None
        pend_nrm = None
        for tt in range(NT):
            par = tt % 2
            pbase = 0 if par == 0 else 3
            W("tensor", "v", ev_set[par])
            for e in range(8):
                for (wdst, pb) in ((Wqh, 0), (Wkh, 1), (Wvh, 2)):
                    t = sc.do("tensor", lambda E, wdst=wdst, pb=pb + pbase, e=e, tt=tt: E.matmul(
                        bank(pb, cols=320), yT[:, e, tt * 128:(tt + 1) * 128], wdst[:, e, :],
                        start=(e == 0), stop=(e == 7)), "t")
            if pend_tr is not None:
                ptt = pend_tr
                W("tensor", "v", gate_done[1 - par])
                W("tensor", "s", s_tr)
                tq = sc.do("tensor", lambda E, ptt=ptt: E.transpose(
                    bankb(6, p1=64, cols=128), qgh[:, ptt, :], ident_b), "t")
                tk = sc.do("tensor", lambda E, ptt=ptt: E.transpose(
                    bankb(7, p1=64, cols=128), kgh[:, ptt, :], ident_b), "t")
                W("scalar", "t", tk)
                sc.do("scalar", lambda E, hp=hp, hl=hl, ptt=ptt: E.copy(
                    qTp[hl * 64:(hl + 1) * 64, hp, ptt * 128:(ptt + 1) * 128],
                    bankb(6, p1=64, cols=128)), "s")
                s_tr = sc.do("scalar", lambda E, hp=hp, hl=hl, ptt=ptt: E.copy(
                    kTown[hl * 64:(hl + 1) * 64, hp, ptt * 128:(ptt + 1) * 128],
                    bankb(7, p1=64, cols=128)), "s")
            pend_tr = tt
            if pend_nrm is not None:
                ntt, npar = pend_nrm, (1 - par)
                W("scalar", "v", gate_done[npar])
                sc.do("scalar", lambda E, ntt=ntt, npar=npar: E.activation(
                    sqscr[:], qgh[:, ntt, :], AF.Square, accum_out=nqt[:, npar:npar + 1]), "s")
                s_nrm = sc.do("scalar", lambda E, ntt=ntt, npar=npar, h=h: E.activation(
                    sqscr2[:], kgh[:, ntt, :], AF.Square, accum_out=nkt[:, npar:npar + 1]), "s")
                W("vector", "s", s_nrm)
                if first_nrm:
                    sc.dep("vector", lambda E, h=h, npar=npar: E.tensor_copy(nq[:, h:h + 1], nqt[:, npar:npar + 1]))
                    v_nrm = sc.dep("vector", lambda E, h=h, npar=npar: E.tensor_copy(nk[:, h:h + 1], nkt[:, npar:npar + 1]))
                    first_nrm = False
                else:
                    sc.dep("vector", lambda E, h=h, npar=npar: E.tensor_tensor(
                        nq[:, h:h + 1], nq[:, h:h + 1], nqt[:, npar:npar + 1], ALU.max))
                    v_nrm = sc.dep("vector", lambda E, h=h, npar=npar: E.tensor_tensor(
                        nk[:, h:h + 1], nk[:, h:h + 1], nkt[:, npar:npar + 1], ALU.max))
            W("vector", "t", t)
            for (j, gates, dst) in ((0, sdg, qgh[:, tt, :]), (1, ssg, kgh[:, tt, :]),
                                    (2, ssg, vga[:, tt, h * 65: h * 65 + 64])):
                sc.dep("vector", lambda E, j=j, gates=gates, tt=tt, h=h, pbase=pbase:
                       E.tensor_tensor(
                           gtmp[:],
                           bank(pbase + j, cols=320).rearrange("p (e d) -> p e d", e=5),
                           gates[:, tt, h * 5: h * 5 + 5].rearrange("p (e o) -> p e o", o=1)
                           .broadcast_to([128, 5, 64]),
                           ALU.mult))
                def _red(E, dst=dst):
                    with E.bass.allow_low_precision(reason="5-term gate sum; fp32 internal"):
                        return E.reduce_sum(dst.rearrange("p (d o) -> p d o", o=1),
                                            gtmp[:].rearrange("p e d -> p d e"), axis=AXX)
                ev = sc.dep("vector", _red)
            ev_set[par] = sc.cnt["v"] - 1
            gate_done[par] = sc.cnt["v"]
            pend_nrm = tt
        ptt = pend_tr
        W("tensor", "v", gate_done[ptt % 2])
        W("tensor", "s", s_tr)
        tq = sc.do("tensor", lambda E, ptt=ptt: E.transpose(
            bankb(6, p1=64, cols=128), qgh[:, ptt, :], ident_b), "t")
        tk = sc.do("tensor", lambda E, ptt=ptt: E.transpose(
            bankb(7, p1=64, cols=128), kgh[:, ptt, :], ident_b), "t")
        W("scalar", "t", tk)
        sc.do("scalar", lambda E, hp=hp, hl=hl, ptt=ptt: E.copy(
            qTp[hl * 64:(hl + 1) * 64, hp, ptt * 128:(ptt + 1) * 128],
            bankb(6, p1=64, cols=128)), "s")
        s_tr = sc.do("scalar", lambda E, hp=hp, hl=hl, ptt=ptt: E.copy(
            kTown[hl * 64:(hl + 1) * 64, hp, ptt * 128:(ptt + 1) * 128],
            bankb(7, p1=64, cols=128)), "s")
        ntt, npar = pend_nrm, pend_nrm % 2
        W("scalar", "v", gate_done[npar])
        sc.do("scalar", lambda E, ntt=ntt, npar=npar: E.activation(
            sqscr[:], qgh[:, ntt, :], AF.Square, accum_out=nqt[:, npar:npar + 1]), "s")
        s_nrm = sc.do("scalar", lambda E, ntt=ntt, npar=npar: E.activation(
            sqscr2[:], kgh[:, ntt, :], AF.Square, accum_out=nkt[:, npar:npar + 1]), "s")
        W("vector", "s", s_nrm)
        sc.dep("vector", lambda E, h=h, npar=npar: E.tensor_tensor(
            nq[:, h:h + 1], nq[:, h:h + 1], nqt[:, npar:npar + 1], ALU.max))
        v_nrm = sc.dep("vector", lambda E, h=h, npar=npar: E.tensor_tensor(
            nk[:, h:h + 1], nk[:, h:h + 1], nkt[:, npar:npar + 1], ALU.max))
        t_h_done[h] = sc.cnt["t"]

    W("tensor", "v", v_nrm)
    t = sc.do("tensor", lambda E: E.transpose(bank(0, p1=8, cols=128), nq[:], ident_f), "t")
    t2 = sc.do("tensor", lambda E: E.transpose(bank(1, p1=8, cols=128), nk[:], ident_f), "t")
    W("vector", "t", t2)
    sc.dep("vector", lambda E: E.tensor_copy(s8a[:], bank(0, p1=8, cols=128)))
    sc.dep("vector", lambda E: E.tensor_copy(s8b[:], bank(1, p1=8, cols=128)))
    sc.dep("vector", lambda E: E.reduce_max(r8a[:], s8a[:], axis=AXX))
    v = sc.dep("vector", lambda E: E.reduce_max(r8b[:], s8b[:], axis=AXX))
    v = sc.dep("vector", lambda E: E.tensor_tensor(r8a[:], r8a[:], r8b[:], ALU.mult))
    W("scalar", "v", v)
    s = sc.do("scalar", lambda E: E.activation(r8b[:], r8a[:], AF.Sqrt), "s")
    W("tensor", "s", s)
    t = sc.do("tensor", lambda E: E.transpose(bank(2, p1=1, cols=8), r8b[0:8, :], ident_f[0:8, 0:8]), "t")
    W("vector", "t", t)
    sc.dep("vector", lambda E: E.tensor_copy(row8[:], bank(2, p1=1, cols=8)))
    v = sc.dep("vector", lambda E: E.tensor_scalar(row8b[:], row8[:], -1.5, None, ALU.mult))
    v = sc.dep("vector", lambda E: E.tensor_copy(row8c[:], row8b[:]))
    W("tensor", "v", v)
    t = sc.do("tensor", lambda E: E.matmul(bank(2, cols=8), ones_b[:, :], row8c[:],
                                           start=True, stop=True), "t")
    W("vector", "t", t)
    v_mh = sc.dep("vector", lambda E: E.tensor_copy(mh[:], bank(2, cols=8)))
    if "mh" in dbg:
        W("gpsimd", "v", v_mh)
        sc.dma("gpsimd", dbg["mh"][:, :], mh[:], "gd")
    if "kTown" in dbg:
        W("gpsimd", "s", s_tr)
        sc.dma("gpsimd", dbg["kTown"][:, :], kTown[:].rearrange("p a b -> p (a b)"), "gd")
    if "vown" in dbg:
        W("gpsimd", "v", v_mh)
        W("gpsimd", "g", g_ones)
        sc.dma("gpsimd", dbg["vown"][:, :], vga[:].rearrange("p a b -> p (a b)"), "gd")

    W("gpsimd", "v", v_mh)
    W("gpsimd", "s", s_tr)
    W("gpsimd", "g", g_ones)
    ktr = ag_in[0:1, 0:KT_ELS].rearrange("o (hp p c) -> o hp p c", hp=4, p=128)
    g = sc.dma("gpsimd", ktr[0].rearrange("hp p c -> p hp c"), kTown[:], "gd")
    vtr = ag_in[0:1, KT_ELS:].rearrange("o (a p c) -> o a p c", a=NT, p=128)
    g = sc.dma("gpsimd", vtr[0].rearrange("a p c -> p a c"), vga[:], "gd")
    W("gpsimd", "gd", g)
    g_ag = sc.do("gpsimd", lambda E: E.collective_compute(
        "AllGather", ALU.bypass, replica_groups=GROUPS,
        ins=[ag_in.opt()], outs=[ag_out.opt()]), "cc", 1)

    sc.barrier()
    if phases < 4:
        _finish(sc)
        return nc, dbg

    accF = bview(ar1, 0, F32, [65, 16384])
    Ab = [[bview(ar2, int(48.25 * KB) + (2 * hl + pa) * 3 * KB, BF16, [128, 1536])
           for pa in range(2)] for hl in range(2)]
    vr = [bview(ar1, 64 * KB, BF16, [128, NT, H * 65]),
          bview(ar2, 32 * KB, BF16, [128, NT, H * 65])]
    kTr = [bview(ar2, 0, BF16, [128, 4, 2048]),
           bview(ar2, 16 * KB, BF16, [128, 4, 2048])]

    W("sync", "cc", g_ag)
    CHF = [(0, 3), (3, 3), (6, 3), (9, 3), (12, 3), (15, 1)]
    s_bank_last = {0: 0, 3: 0}
    v_accf = 0
    g_kv = {}
    t_r_done = {}
    for r in range(4):
        pr = r % 2
        if r == 0:
            for rr in (0, 1):
                ktro = ag_out[rr:rr + 1, 0:KT_ELS].rearrange("o (hp p c) -> o hp p c", hp=4, p=128)
                sc.dma("sync", kTr[rr % 2][:], ktro[0].rearrange("hp p c -> p hp c"), f"ldk{rr % 2}")
                vtro = ag_out[rr:rr + 1, KT_ELS:].rearrange("o (a p c) -> o a p c", a=NT, p=128)
                g_kv[rr] = sc.dma("sync", vr[rr % 2][:], vtro[0].rearrange("a p c -> p a c"), f"ldk{rr % 2}")
        elif r < 3:
            W("sync", "t", t_r_done[r - 1])
            rr = r + 1
            ktro = ag_out[rr:rr + 1, 0:KT_ELS].rearrange("o (hp p c) -> o hp p c", hp=4, p=128)
            sc.dma("sync", kTr[rr % 2][:], ktro[0].rearrange("hp p c -> p hp c"), f"ldk{rr % 2}")
            vtro = ag_out[rr:rr + 1, KT_ELS:].rearrange("o (a p c) -> o a p c", a=NT, p=128)
            g_kv[rr] = sc.dma("sync", vr[rr % 2][:], vtro[0].rearrange("a p c -> p a c"), f"ldk{rr % 2}")
        W("tensor", f"ldk{r % 2}", g_kv[r])
        if r == 0:
            W("scalar", "v", v_mh)
        for hp in range(4):
            for ib in range(4):
                s_exp_at = {}
                t_av_buf = {(0, 0): 0, (0, 1): 0, (1, 0): 0, (1, 1): 0}
                t_mm = {}
                bset = {}
                for ci, (j0, nj) in enumerate(CHF):
                    for hl in range(2):
                        b0 = 3 * hl
                        bset[(hl, ci)] = b0
                        W("tensor", "s", s_bank_last.get(b0, 0))
                        for jj in range(nj):
                            t = sc.do("tensor", lambda E, hl=hl, hp=hp, j0=j0, jj=jj, ib=ib, b0=b0, pr=pr: E.matmul(
                                bank(b0 + jj), kTr[pr][hl * 64:(hl + 1) * 64, hp, (j0 + jj) * 128:(j0 + jj + 1) * 128],
                                qTp[hl * 64:(hl + 1) * 64, hp, ib * 512:(ib + 1) * 512],
                                start=True, stop=True,
                                tile_position=(hl * 64, 0)), "t")
                        t_mm[hl] = t
                    for hl in range(2):
                        h = hp * 2 + hl
                        b0 = bset[(hl, ci)]
                        abuf = Ab[hl][ci % 2]
                        W("scalar", "t", t_mm[hl])
                        W("scalar", "t", t_av_buf[(hl, ci % 2)])
                        s_e = sc.do("scalar", lambda E, abuf=abuf, b0=b0, nj=nj, h=h: E.activation(
                            abuf[:, 0: nj * 512], bank(b0, cols=nj * 512), AF.Exp,
                            bias=mh[:, h:h + 1]), "s")
                        s_bank_last[b0] = s_e
                        s_exp_at[(hl, ci)] = s_e
                    for hl in range(2):
                        if ci == 0:
                            continue
                        _emit_av(sc, W, bank, vr, pr, qTp, accF, CHF, ci - 1, hl, hp,
                                 s_exp_at, t_av_buf, Ab, v_accf)
                for hl in range(2):
                    _emit_av(sc, W, bank, vr, pr, qTp, accF, CHF, 5, hl, hp,
                             s_exp_at, t_av_buf, Ab, v_accf)
                for hl in range(2):
                    h = hp * 2 + hl
                    W("vector", "t", t_av_buf[(hl, 5 % 2)])
                    col = (h * 4 + ib) * 512
                    if r == 0:
                        v_accf = sc.do("vector", lambda E, hl=hl, col=col: E.tensor_copy(
                            accF[:, col: col + 512], bank(6 + hl, p1=65)), "v")
                    else:
                        v_accf = sc.do("vector", lambda E, hl=hl, col=col: E.tensor_tensor(
                            accF[:, col: col + 512], bank(6 + hl, p1=65),
                            accF[:, col: col + 512], ALU.add), "v")
        t_r_done[r] = sc.cnt["t"]
    if "accF" in dbg:
        W("gpsimd", "v", v_accf)
        sc.dma("gpsimd", dbg["accF"][:, :], accF[:], "gd")

    sc.barrier()

    oaf = bview(ar2, 0, BF16, [128, NT, H * 64])
    zbuf = bview(ar2, 32 * KB, BF16, [128, 2560])
    zT = bview(ar2, 40 * KB, BF16, [128, 20, 128])
    WoBb = bview(ar1, 20 * KB, BF16, [128, 20, 1024])

    t_ep = 0
    v_ep = 0
    for h in range(H):
        for tt in range(NT):
            ib, sub = tt // 4, tt % 4
            col = (h * 4 + ib) * 512 + sub * 128
            W("tensor", "v", v_ep)
            t_ep = sc.do("tensor", lambda E, col=col: E.transpose(
                bank(0, p1=128, cols=65), accF[0:65, col:col + 128], ident_f[0:65, 0:65]), "t")
            W("vector", "t", t_ep)
            sc.do("vector", lambda E, h=h, tt=tt: E.tensor_copy(
                oaf[:, tt, h * 64:(h + 1) * 64], bank(0, cols=64)), "v")
            v_ep = sc.do("vector", lambda E, h=h, tt=tt: E.tensor_copy(
                ztok[:, tt, h:h + 1], bank(0, cols=1, off=64)), "v")
    v_rz2 = sc.dep("vector", lambda E: E.reciprocal(
        rztok[:].rearrange("p a b -> p (a b)"), ztok[:].rearrange("p a b -> p (a b)")))
    for tt in range(NT):
        v_mz = sc.dep("vector", lambda E, tt=tt: E.tensor_tensor(
            mdz[:, tt, :].rearrange("p (h e) -> p h e", h=8),
            mdA[:, tt, :].rearrange("p (h e) -> p h e", h=8),
            rztok[:, tt, :].rearrange("p (h e) -> p h e", e=1).broadcast_to([128, 8, 5]),
            ALU.mult))
    if "oaf" in dbg:
        W("gpsimd", "v", v_ep)
        sc.dma("gpsimd", dbg["oaf"][:, :], oaf[:].rearrange("p a b -> p (a b)"), "gd")
    if "rzt" in dbg:
        W("gpsimd", "v", v_rz2)
        sc.dma("gpsimd", dbg["rzt"][:, :], rztok[:].rearrange("p a b -> p (a b)"), "gd")

    W("sync", "v", max(v_ep, v_rz2))
    for kc in range(20):
        staged(WoB[kc * 128:(kc + 1) * 128, :], WoBb[:, kc, :], kc, stgB[kc % 2][:, 0:1024])
    v_wo = max(cast_m[0], cast_m[1])

    yf2 = bview(ar1, 60 * KB, F32, [128, 1024])
    res2 = bview(ar1, 64 * KB, F32, [128, 1024])
    sq2 = bview(ar1, 68 * KB, F32, [128, 1024])
    out_t = bview(ar1, 72 * KB, F32, [128, 1024])

    outr = out.rearrange("(a p) d -> a p d", p=128)
    zT2 = [zT, bview(ar2, 45 * KB, BF16, [128, 20, 128])]
    zbuf2 = [zbuf, bview(ar2, 50 * KB, BF16, [128, 2560])]
    s_o = 0
    g_out = 0
    v_res = 0
    v_zt = {0: 0, 1: 0, 2: 0, 3: 0}
    t_wo = {0: 0, 1: 0}
    for tt in range(NT):
        par = tt % 2
        zb, zt_ = zbuf2[par], zT2[par]
        W("vector", "v", v_mz)
        W("vector", "t", t_wo[par])
        v_z = sc.dep("vector", lambda E, tt=tt, zb=zb: E.tensor_tensor(
            zb[:].rearrange("p (h e d) -> p h e d", h=8, e=5),
            oaf[:, tt, :].rearrange("p (h o d) -> p h o d", h=8, o=1).broadcast_to([128, 8, 5, 64]),
            mdz[:, tt, :].rearrange("p (h e o) -> p h e o", h=8, o=1).broadcast_to([128, 8, 5, 64]),
            ALU.mult))
        for kc in range(20):
            tb = 4 + (kc % 4)
            W("tensor", "v", v_zt[kc % 4] if kc >= 4 else v_z)
            W("tensor", "v", v_zt[kc % 4])
            t = sc.do("tensor", lambda E, kc=kc, tb=tb, zb=zb: E.transpose(
                bankb(tb, cols=128), zb[:, kc * 128:(kc + 1) * 128], ident_b), "t")
            W("vector", "t", t)
            v_zt[kc % 4] = sc.do("vector", lambda E, kc=kc, tb=tb, zt_=zt_: E.tensor_copy(
                zt_[:, kc, :], bankb(tb, cols=128)), "v")
        v_zt_done = max(v_zt.values())
        W("tensor", "v", v_zt_done)
        W("tensor", "g", v_wo)
        W("tensor", "v", v_res)
        for mc in range(2):
            for kc in range(20):
                t = sc.do("tensor", lambda E, mc=mc, kc=kc, par=par, zt_=zt_: E.matmul(
                    bank(2 * par + mc, cols=512), zt_[:, kc, :], WoBb[:, kc, mc * 512:(mc + 1) * 512],
                    start=(kc == 0), stop=(kc == 19)), "t")
        t_wo[par] = t
        W("sync", "v", v_res)
        vx = sc.dma("sync", yf2[:], ydr[tt], "ldm")
        W("vector", "ldm", vx)
        W("vector", "t", t)
        W("vector", "s", s_o)
        for mc in range(2):
            v_res = sc.do("vector", lambda E, mc=mc, par=par: E.tensor_tensor(
                res2[:, mc * 512:(mc + 1) * 512], bank(2 * par + mc, cols=512),
                yf2[:, mc * 512:(mc + 1) * 512], ALU.add), "v")
        sc.dep("vector", lambda E: E.reduce_sum(lnA[:], res2[:], axis=AXX))
        W("scalar", "v", sc.cnt["v"])
        W("scalar", "s", s_o)
        s_sq = sc.do("scalar", lambda E: E.activation(sq2[:], res2[:], AF.Square,
                                                      accum_out=lnB[:]), "s")
        W("vector", "s", s_sq)
        sc.dep("vector", lambda E: E.tensor_scalar(lnC[:], lnA[:], 1.0 / D, None, ALU.mult))
        sc.dep("vector", lambda E: E.tensor_scalar(lnD[:], lnB[:], 1.0 / D, None, ALU.mult))
        sc.dep("vector", lambda E: E.tensor_tensor(lnG[:], lnC[:], lnC[:], ALU.mult))
        v = sc.dep("vector", lambda E: E.tensor_tensor(lnD[:], lnD[:], lnG[:], ALU.subtract))
        W("scalar", "v", v)
        W("scalar", "s", s_sq)
        ss = sc.do("scalar", lambda E: E.activation(lnD[:], lnD[:], AF.Sqrt, bias=epsc[:, 0:1]), "s")
        W("vector", "s", ss)
        sc.dep("vector", lambda E: E.reciprocal(lnE_[:], lnD[:]))
        sc.dep("vector", lambda E: E.tensor_tensor(lnF[:], lnC[:], lnE_[:], ALU.mult))
        v_ln = sc.dep("vector", lambda E: E.tensor_scalar(lnF[:], lnF[:], -1.0, None, ALU.mult))
        W("scalar", "v", v_ln)
        W("scalar", "gd", g_out)
        s_o = sc.do("scalar", lambda E: E.activation(out_t[:], res2[:], AF.Identity,
                                                     bias=lnF[:, 0:1], scale=lnE_[:, 0:1]), "s")
        W("gpsimd", "s", s_o)
        g_out = sc.dma("gpsimd", outr[tt], out_t[:], "gd")

    _finish(sc)
    return nc, dbg


def _finish(sc):
    sc.wait("gpsimd", "gd", sc.cnt.get("gd", 0))
    sc.replay()


def kernel(**inputs):
    x = inputs["L0_attn_x"]
    Wq, Wk, Wv = inputs["L0_attn_Wq"], inputs["L0_attn_Wk"], inputs["L0_attn_Wv"]
    Wqf, Wkvf = inputs["L0_ffn_Wq"], inputs["L0_ffn_Wkv"]
    Ws, Wd, Wo = inputs["L0_ffn_Ws"], inputs["L0_ffn_Wd"], inputs["L0_ffn_Wo"]

    WaT = np.ascontiguousarray(np.concatenate([Wq.T, Wk.T, Wv.T], axis=1)).astype(np.float32)
    Wsd = np.ascontiguousarray(np.concatenate([Ws, Wd], axis=1)).astype(np.float32)
    WoB = np.ascontiguousarray(
        np.tile(np.asarray(Wo).reshape(DH, E, D).transpose(1, 0, 2).reshape(E * DH, D), (H, 1)))

    nc, dbg = build()
    in_maps = []
    for c in range(8):
        b, q = c // 4, c % 4
        xb = np.asarray(x[b], dtype=np.float32)
        xTb = np.ascontiguousarray(xb.T)
        in_maps.append({
            "xT": xTb,
            "xqT": np.ascontiguousarray(xTb[:, q * SQ:(q + 1) * SQ]),
            "xq": np.ascontiguousarray(xb[q * SQ:(q + 1) * SQ]),
            "WaT": WaT, "Wsd": Wsd,
            "Wqf": np.ascontiguousarray(np.asarray(Wqf, dtype=np.float32)),
            "Wkvf": np.ascontiguousarray(np.asarray(Wkvf, dtype=np.float32)),
            "WoB": WoB.astype(np.float32),
        })
    res = run_bass_kernel_spmd(nc, in_maps, core_ids=list(range(8)))
    outp = np.empty((2, S, D), dtype=np.float32)
    for c in range(8):
        b, q = c // 4, c % 4
        outp[b, q * SQ:(q + 1) * SQ] = res.results[c]["out"]
    return outp
